# version 01
# baseline (speedup 1.0000x reference)
import numpy as np
import ml_dtypes
import concourse.bass as bass
import concourse.mybir as mybir
from concourse.alu_op_type import AluOpType
from concourse.bass_utils import run_bass_kernel_spmd

F32 = mybir.dt.float32
BF16 = mybir.dt.bfloat16
I32DT = mybir.dt.int32
AF = mybir.ActivationFunctionType
AX = mybir.AxisListType
OP = AluOpType
bfloat16 = ml_dtypes.bfloat16

S = 2048
D = 1024
DBG_HEADS = None
DBG_STOP = None
NT = 16
NC = 8
NH = 16
HD = 64
XM = 1024.0
SHIFT = -5.0
BS_THR = 0.01 * 64 * 64 * 8

ENGINES = ("sync", "tensor", "vector", "scalar", "gpsimd")


EPOCH = 3584


class Sched:
    def __init__(self):
        self.q = {e: [] for e in ENGINES}
        self.ctr = {"dma_sync": 0, "dma_gpsimd": 0, "t": 0, "s": 0, "v": 0}
        self.last_wait = {e: {} for e in ENGINES}
        self.sems = None

    def op(self, eng, fn, inc=None, waits=()):
        for ev in waits:
            if ev is None:
                continue
            semname, val = ev[0], ev[1]
            if val <= 0:
                continue
            if val > self.last_wait[eng].get(semname, 0):
                self.last_wait[eng][semname] = val
                self.q[eng].append(("wait", semname, val))
        ret = None
        if inc is not None:
            semname, amt = inc
            if (self.ctr[semname] % EPOCH) + amt > EPOCH:
                self.ctr[semname] = ((self.ctr[semname] + EPOCH - 1) // EPOCH) * EPOCH
            self.ctr[semname] += amt
            ret = (semname, self.ctr[semname], amt)
        self.q[eng].append(("op", fn, ret))
        return ret

    def dma(self, eng, out, in_, waits=()):
        eng = "sync"
        sem = "dma_" + eng
        return self.op(eng, lambda E, o=out, i=in_: E.dma_start(out=o, in_=i),
                       inc=(sem, 16), waits=waits)

    def align_dma_waits(self):
        run_end = {}
        v = 0
        ends = []
        runs = []
        cur = []
        for item in self.q["sync"]:
            if item[0] == "wait":
                if cur:
                    runs.append(cur)
                    cur = []
            elif item[2] is not None and item[2][0] == "dma_sync":
                cur.append(item[2][1])
        if cur:
            runs.append(cur)
        for r in runs:
            end = r[-1]
            for val in r:
                run_end[val] = end
        for e in ENGINES:
            newq = []
            maxw = {}
            for item in self.q[e]:
                if item[0] == "wait" and item[1] == "dma_sync":
                    val = run_end.get(item[2], item[2])
                    if val <= maxw.get("dma_sync", 0):
                        continue
                    maxw["dma_sync"] = val
                    newq.append(("wait", "dma_sync", val))
                else:
                    if item[0] == "wait":
                        if item[2] <= maxw.get(item[1], 0):
                            continue
                        maxw[item[1]] = item[2]
                    newq.append(item)
            self.q[e] = newq

    def check(self):
        pos = {e: 0 for e in ENGINES}
        val = {sn: 0 for sn in self.ctr}
        progress = True
        while progress:
            progress = False
            for e in ENGINES:
                while pos[e] < len(self.q[e]):
                    item = self.q[e][pos[e]]
                    if item[0] == "wait":
                        if val[item[1]] < item[2]:
                            break
                    else:
                        if item[2] is not None:
                            val[item[2][0]] = item[2][1]
                    pos[e] += 1
                    progress = True
        stuck = {e: (pos[e], len(self.q[e])) for e in ENGINES if pos[e] < len(self.q[e])}
        if stuck:
            detail = {e: self.q[e][pos[e]][:3] for e in stuck}
            raise RuntimeError(f"schedule deadlock: {stuck} blocked_on={detail}")

    def _phys(self, semname, val):
        k = (val - 1) // EPOCH
        return self.sems[semname][k], val - k * EPOCH

    def emit(self, block):
        for name, lst in self.sems.items():
            need = (self.ctr[name] + EPOCH - 1) // EPOCH
            assert need <= len(lst), (name, self.ctr[name], len(lst))
        for e in ENGINES:
            dec = getattr(block, e)

            def body(E, e=e):
                for item in self.q[e]:
                    if item[0] == "wait":
                        ps, lv = self._phys(item[1], item[2])
                        E.wait_ge(ps, lv)
                    else:
                        instr = item[1](E)
                        if item[2] is not None:
                            ps, _ = self._phys(item[2][0], item[2][1])
                            instr.then_inc(ps, item[2][2])
            dec(body)


def ts(i, n):
    return slice(i * n, (i + 1) * n)


SCALARS = [0.125, SHIFT, 0.01, 1.0 / D, 1e-5, 1.0 / 127.0, 1e-12, BS_THR,
           -8.0, 1.5, -0.5, 0.5, 128.0, -128.0, 127.0, -127.0, -1024.0, 1.0,
           2.0, 64.0, 1.0 / 64.0, -1.0, 0.0]


def host_consts():
    cf = np.zeros((128, 256), dtype=np.float32)
    cf[:, 0:128] = np.eye(128, dtype=np.float32)
    cf[0:32, 128:160] = np.eye(32, dtype=np.float32)
    cf[0:32, 160:192] = 1.0 - np.eye(32, dtype=np.float32)
    for i, v in enumerate(SCALARS):
        cf[:, 192 + i] = v
    cb = np.zeros((128, 4096), dtype=np.float32)
    j = np.arange(2048)
    for c in range(32):
        cb[c, 0:2048] = (j // 64 == c).astype(np.float32)
    e64 = 1.0 - np.eye(64, dtype=np.float32)
    cb[0:64, 2048:2112] = e64
    cb[64:128, 2112:2176] = e64
    p = np.arange(128)
    for a in range(32):
        cb[a, 2176:2304] = ((p < 64) if a % 2 == 0 else (p >= 64)).astype(np.float32)
        cb[a, 2304:2320] = (np.arange(16) == a // 2).astype(np.float32)
    for t in range(16):
        for a in range(32):
            cb[:, 2320 + t * 32 + a] = (a == 2 * t + p // 64).astype(np.float32)
    return cf, cb


def build_program(stages=("attn_a", "ffn", "attn_b", "ffn"), nseq=2):
    nc = bass.Bass("TRN2", target_bir_lowering=False,
                   detect_race_conditions=False)
    xin = nc.dram_tensor("xin", [nseq * S, D], F32, kind="ExternalInput").ap()
    wdr = {}
    for nm in ("wq_a", "wk_a", "wv_a", "wo_a", "wq_b", "wk_b", "wv_b", "wo_b"):
        wdr[nm] = nc.dram_tensor(nm, [D, D], F32, kind="ExternalInput").ap()
    frhsT = nc.dram_tensor("frhsT", [D, D], F32, kind="ExternalInput").ap()
    cfD = nc.dram_tensor("cf", [128, 256], F32, kind="ExternalInput").ap()
    cbD = nc.dram_tensor("cb", [128, 4096], F32, kind="ExternalInput").ap()
    xout = nc.dram_tensor("xout", [nseq * S, D], F32, kind="ExternalOutput").ap()
    scr = nc.dram_tensor("scr_rows", [32, 128], BF16, kind="Internal").ap()

    sch = Sched()
    import contextlib
    est = contextlib.ExitStack()
    with est:
        sb = lambda name, shape, dt: est.enter_context(nc.sbuf_tensor(name, shape, dt))
        CF = sb("CF", [128, 216], F32)
        CBT = sb("CBT", [128, 2832], BF16)
        X = sb("X", [128, NT * D], BF16)
        XNT = sb("XNT", [128, NC * S], BF16)
        VT = sb("VT", [128, NT * D], BF16)
        CXT = sb("CXT", [128, NC * S], BF16)
        QP = sb("QP", [128, S], BF16)
        KP = sb("KP", [128, S], BF16)
        WBF = sb("WBF", [128, NC * D], BF16)
        WST2 = sb("WST2", [128, D], F32)
        WBF2 = sb("WBF2", [128, D], BF16)
        WBF3 = sb("WBF3", [128, D], BF16)
        ET = sb("ET", [128, 2 * S], BF16)
        EA = sb("EA", [128, S], BF16)
        Q98 = sb("Q98", [98, S], BF16)
        K98 = sb("K98", [98, S], BF16)
        SIG = sb("SIG", [128, NT], F32)
        LSA = sb("LSA", [128, NT], F32)
        SMALL = sb("SMALL", [128, 224], F32)
        DPAD = sb("DPAD", [128, 4], F32)
        SM2 = sb("SM2", [128, 296], F32)
        SMB = sb("SMB", [128, 544], BF16)
        XN = sb("XN", [128, D], F32)
        XSC = sb("XSC", [128, D], F32)
        XI = sb("XI", [128, D], I32DT)
        XNB = sb("XNB", [128, 2048], BF16)
        MSCR = ET[:, 0:1024]
        QRHS = WBF
        BCR = EA[:, 0:2048]
        RROW1 = XNB[0:1, 0:1024]
        RROW2 = XNB[0:1, 1024:2048]
        XOUTS = XN
        RSB = sb("RSB", [128, D], BF16)
        PA = est.enter_context(nc.psum_tensor("PA", [128, 1024], F32))
        PB = est.enter_context(nc.psum_tensor("PB", [128, 1024], F32))
        PC = est.enter_context(nc.psum_tensor("PC", [128, 2048], F32))

        sems = {}
        for sn, cnt in (("dma_sync", 10), ("dma_gpsimd", 1), ("t", 10), ("s", 4),
                        ("v", 16)):
            sems[sn] = [est.enter_context(nc.semaphore(f"{sn}_{i}"))
                        for i in range(cnt)]
        sch.sems = sems

        for i, v in enumerate(SCALARS):
            nc.const_aps.register(F32, float(v), CF[:, 192 + i:193 + i])

        ID128F = CF[:, 0:128]
        I32F = CF[0:32, 128:160]
        OMI32F = CF[0:32, 160:192]
        SELK = CBT[0:32, 0:2048]
        EYE2 = CBT[0:128, 2048:2176]
        LC = CBT[0:32, 2176:2304]
        E2C = CBT[0:32, 2304:2320]
        SELT = CBT[0:128, 2320:2832]
        ONESC = SMB[:, 112:113]
        ONESR = SMB[0:1, 113:241]
        DGR = SMB[:, 256:384]
        VHB = SMB[:, 384:400]

        blk = est.enter_context(nc.Block())
        SY, T, V, SE, G = "sync", "tensor", "vector", "scalar", "gpsimd"

        def vop(fn, waits=()):
            return sch.op(V, fn, inc=("v", 1), waits=waits)

        def top(fn, waits=()):
            return sch.op(T, fn, inc=("t", 1), waits=waits)

        def sop(fn, waits=()):
            return sch.op(SE, fn, inc=("s", 1), waits=waits)

        def vpad():
            sch.op(V, lambda E: E.tensor_copy(DPAD[:, 0:2], DPAD[:, 2:4]), inc=None)

        ev_cf = sch.dma(SY, CF[:, :], cfD[:, 0:216])
        ev_cb = None
        for q4 in range(3):
            w = 1024 if q4 < 2 else 2832 - 2048
            evd = sch.dma(SY, WST2[:, 0:w], cbD[:, q4 * 1024:q4 * 1024 + w],
                          waits=[ev_cb])
            ev_cb = vop(lambda E, q4=q4, w=w: E.tensor_copy(
                CBT[:, q4 * 1024:q4 * 1024 + w], WST2[:, 0:w]), waits=[evd])
        ev_consts = ev_cb
        vop(lambda E: E.memset(DPAD[:, :], 0.0))
        ev_ones = vop(lambda E: E.memset(SMB[:, 112:113], 1.0))
        ev_ones2 = vop(lambda E: E.memset(SMB[0:1, 113:241], 1.0))

        def build_ffn_weights(ev_enter):
            ev_prev = [e for e in (ev_consts, ev_enter) if e is not None][-1]
            ev_evac = {}
            for fc in range(NC):
                evd = sch.dma(SY, WST2[:, :], frhsT[ts(fc, 128), :], waits=[ev_prev])
                vop(lambda E, fc=fc: E.tensor_reduce(
                    SMALL[:, fc:fc + 1], WST2[:, :], AX.X, OP.max,
                    apply_absolute_value=True), waits=[evd])
                vpad()
                vop(lambda E, fc=fc: E.tensor_scalar(
                    SMALL[:, 16 + fc:17 + fc], SMALL[:, fc:fc + 1], 1e-12, 1.0 / 127.0,
                    OP.max, OP.mult))
                vpad()
                vop(lambda E, fc=fc: E.reciprocal(
                    SMALL[:, 32 + fc:33 + fc], SMALL[:, 16 + fc:17 + fc]))
                vpad()
                ev_prev = vop(lambda E, fc=fc: E.tensor_scalar(
                    XSC[:, :], WST2[:, :], SMALL[:, 32 + fc:33 + fc], None, OP.mult))
                vop(lambda E: E.tensor_scalar(XNB[:, 0:1024], XSC[:, :], 0.0, 128.0,
                                              OP.max, OP.add))
                vop(lambda E: E.tensor_scalar(MSCR[:, :], XSC[:, :], 0.0, -128.0,
                                              OP.min, OP.add))
                vop(lambda E: E.tensor_tensor(XNB[:, 0:1024], XNB[:, 0:1024], MSCR[:, :], OP.add))
                vop(lambda E: E.tensor_scalar(XNB[:, 0:1024], XNB[:, 0:1024], -127.0, 127.0,
                                              OP.max, OP.min))
                vop(lambda E: E.tensor_copy(XI[:, :], XNB[:, 0:1024]))
                e = vop(lambda E: E.tensor_copy(XSC[:, :], XI[:, :]))
                last_t = None
                for dc in range(NC):
                    last_t = top(lambda E, fc=fc, dc=dc: E.transpose(
                        (PA if fc % 2 == 0 else PB).ap()[:, ts(dc, 128)],
                        XSC[:, ts(dc, 128)], ID128F),
                        waits=[e, ev_cf, ev_evac.get(fc - 2), ev_enter])
                ev_evac[fc] = vop(lambda E, fc=fc: E.tensor_copy(
                    QRHS.ap().rearrange("p (c n) -> p c n", n=D)[:, :, ts(0, 1) if False else slice(fc * 128, (fc + 1) * 128)],
                    (PA if fc % 2 == 0 else PB).ap().rearrange("p (c n) -> p c n", n=128)),
                    waits=[last_t])
            e = top(lambda E: E.transpose(PC.ap()[0:8, 0:128], SMALL[:, 16:24], ID128F),
                    waits=[ev_evac[NC - 1], ev_enter])
            e = vop(lambda E: E.tensor_copy(SMB[0:8, 256:384], PC.ap()[0:8, 0:128]),
                    waits=[e])
            evd = sch.dma(G, scr[0:8, :], SMB[0:8, 256:384], waits=[e])
            evd = sch.dma(G, RROW1.rearrange("o (c p) -> o c p", p=128),
                          scr[0:8, :], waits=[evd])
            em1 = top(lambda E: E.matmul(PA.ap()[:, 0:512], ONESR, RROW1[:, 0:512],
                                         start=True, stop=True),
                      waits=[evd, ev_ones2, ev_evac[NC - 1], ev_evac[NC - 2]])
            em2 = top(lambda E: E.matmul(PB.ap()[:, 0:512], ONESR, RROW1[:, 512:1024],
                                         start=True, stop=True))
            vop(lambda E: E.tensor_copy(RSB[:, 0:512], PA.ap()[:, 0:512]), waits=[em1])
            e2 = vop(lambda E: E.tensor_copy(RSB[:, 512:1024], PB.ap()[:, 0:512]),
                     waits=[em2])
            return e2

        st = {}

        def ln_batch(waits):
            for t in range(NT):
                vop(lambda E, t=t: E.tensor_reduce(SMALL[:, t:t + 1],
                                                   X[:, ts(t, 1024)], AX.X, OP.add),
                    waits=waits if t == 0 else ())
            for t in range(NT):
                vop(lambda E, t=t: E.tensor_tensor(XNB[:, 0:1024], X[:, ts(t, 1024)],
                                                   X[:, ts(t, 1024)], OP.mult))
                vop(lambda E, t=t: E.tensor_reduce(SMALL[:, 16 + t:17 + t],
                                                   XNB[:, 0:1024], AX.X, OP.add))
            for t in range(NT):
                vop(lambda E, t=t: E.tensor_scalar(SMALL[:, 32 + t:33 + t],
                                                   SMALL[:, t:t + 1],
                                                   1.0 / D, None, OP.mult))
            for t in range(NT):
                vop(lambda E, t=t: E.tensor_scalar(SMALL[:, 48 + t:49 + t],
                                                   SMALL[:, 16 + t:17 + t],
                                                   1.0 / D, None, OP.mult))
            for t in range(NT):
                vop(lambda E, t=t: E.tensor_tensor(SMALL[:, 64 + t:65 + t],
                                                   SMALL[:, 32 + t:33 + t],
                                                   SMALL[:, 32 + t:33 + t], OP.mult))
            for t in range(NT):
                vop(lambda E, t=t: E.tensor_tensor(SMALL[:, 80 + t:81 + t],
                                                   SMALL[:, 48 + t:49 + t],
                                                   SMALL[:, 64 + t:65 + t],
                                                   OP.subtract))
            for t in range(NT):
                vop(lambda E, t=t: E.tensor_scalar(SMALL[:, 96 + t:97 + t],
                                                   SMALL[:, 80 + t:81 + t],
                                                   1e-5, None, OP.add))
            for t in range(NT):
                e = vop(lambda E, t=t: E.reciprocal(SMALL[:, 112 + t:113 + t],
                                                    SMALL[:, 96 + t:97 + t]))
            eg = sop(lambda E: E.activation(SMALL[:, 128:144], SMALL[:, 112:128],
                                            AF.Sqrt), waits=[e, ev_cf])
            for t in range(NT):
                vop(lambda E, t=t: E.tensor_tensor(SMALL[:, 144 + t:145 + t],
                                                   SMALL[:, 128 + t:129 + t],
                                                   SMALL[:, 128 + t:129 + t],
                                                   OP.mult), waits=[eg] if t == 0 else ())
            for t in range(NT):
                vop(lambda E, t=t: E.tensor_tensor(SMALL[:, 160 + t:161 + t],
                                                   SMALL[:, 96 + t:97 + t],
                                                   SMALL[:, 144 + t:145 + t], OP.mult))
            for t in range(NT):
                vop(lambda E, t=t: E.tensor_scalar(SMALL[:, 176 + t:177 + t],
                                                   SMALL[:, 160 + t:161 + t],
                                                   -0.5, 1.5, OP.mult, OP.add))
            for t in range(NT):
                e = vop(lambda E, t=t: E.tensor_tensor(SMALL[:, 192 + t:193 + t],
                                                       SMALL[:, 128 + t:129 + t],
                                                       SMALL[:, 176 + t:177 + t],
                                                       OP.mult))
            return e

        def ln_norm_tile(t, extra=()):
            e = vop(lambda E, t=t: E.tensor_scalar(XN[:, :], X[:, ts(t, 1024)],
                                                   SMALL[:, 32 + t:33 + t],
                                                   SMALL[:, 192 + t:193 + t],
                                                   OP.subtract, OP.mult), waits=extra)
            return e

        def transpose_to(dst, src, t, ev_in, pp, extra_waits=()):
            P = PA if pp == 0 else PB
            last = None
            for dc in range(NC):
                last = top(lambda E, dc=dc, P=P, src=src: E.transpose(
                    P.ap()[:, ts(dc, 128)], src[:, ts(dc, 128)], ID128F),
                    waits=[ev_in, ev_cf] + list(extra_waits))
            e = vop(lambda E, t=t, P=P, dst=dst: E.tensor_copy(
                dst.ap().rearrange("p (c n) -> p c n", n=S)[:, :, ts(t, 128)],
                P.ap().rearrange("p (c n) -> p c n", n=128)), waits=[last])
            return e

        def load_weight_full(wap, waits):
            evs = []
            last = None
            for dc in range(NC):
                evd = sch.dma(SY, WST2[:, :], wap[ts(dc, 128), :],
                              waits=list(waits) + [last])
                last = vop(lambda E, dc=dc: E.tensor_copy(WBF[:, ts(dc, 1024)],
                                                          WST2[:, :]), waits=[evd])
                evs.append(last)
            return evs

        def load_weight_colpair(wap, c, dstbf, waits):
            evd = sch.dma(SY, WST2[:, :].rearrange("p (c8 n) -> p c8 n", n=128),
                          wap[:, ts(c, 128)].rearrange("(c8 p) n -> p c8 n", p=128),
                          waits=waits)
            e = vop(lambda E, dstbf=dstbf: E.tensor_copy(dstbf[:, :], WST2[:, :]),
                    waits=[evd])
            return e

        def attn_a_head(c, h2, h, ev_qp, ev_kp, ev_v):
            b0 = h2 * 64
            ev_exp = {}
            for u in range(NT):
                eb = u % 2
                for qh in range(2):
                    P = PA if qh == 0 else PB
                    for j in range(2):
                        em = top(lambda E, u=u, qh=qh, P=P, b0=b0, j=j: E.matmul(
                            P.ap()[:, ts(j, 512)], KP[b0:b0 + 64, ts(u, 128)],
                            QP[b0:b0 + 64, qh * 1024 + j * 512:
                               qh * 1024 + (j + 1) * 512], start=True, stop=True),
                            waits=[ev_qp, ev_kp, ev_exp.get((u - 1, qh)),
                                   st["evac"].get(h - 1)])
                    ev_exp[(u, qh)] = sop(lambda E, u=u, qh=qh, P=P, eb=eb: E.activation(
                        ET[:, eb * S + qh * 1024: eb * S + (qh + 1) * 1024],
                        P.ap()[:, :], AF.Exp, bias=SHIFT, scale=0.125),
                        waits=[em, ev_cf, st["vdone"].get((h, u - 2))])
                for qc in range(4):
                    top(lambda E, u=u, qc=qc, eb=eb, h=h: E.matmul(
                        PC.ap()[0:64, ts(qc, 512)],
                        VT[:, u * 1024 + h * 64: u * 1024 + (h + 1) * 64],
                        ET[:, eb * S + qc * 512: eb * S + (qc + 1) * 512],
                        start=(u == 0), stop=(u == NT - 1)),
                        waits=[ev_exp[(u, 0)], ev_exp[(u, 1)], ev_v,
                               st["evac"].get(h - 1), ev_ones])
                    st["vdone"][(h, u)] = top(lambda E, qc=qc, eb=eb, u=u: E.matmul(
                        PC.ap()[64:65, ts(qc, 512)], ONESC,
                        ET[:, eb * S + qc * 512: eb * S + (qc + 1) * 512],
                        start=(u == 0), stop=(u == NT - 1), tile_position=(0, 64)))
            def _recip(E, dst, srcap):
                with nc.allow_low_precision(reason="bf16 recip feeds bf16 bcast matmul"):
                    return E.reciprocal(dst, srcap)
            vop(lambda E: _recip(E, RROW1, PC.ap()[64:65, 0:1024]),
                waits=[st["vdone"][(h, NT - 1)]])
            er = vop(lambda E: _recip(E, RROW2, PC.ap()[64:65, 1024:2048]))
            for j in range(2):
                eb1 = top(lambda E, j=j: E.matmul(
                    PA.ap()[:, ts(j, 512)], ONESR, RROW1[:, ts(j, 512)],
                    start=True, stop=True),
                    waits=[er, ev_exp[(NT - 1, 0)], ev_ones2])
            for j in range(2):
                eb2 = top(lambda E, j=j: E.matmul(
                    PB.ap()[:, ts(j, 512)], ONESR, RROW2[:, ts(j, 512)],
                    start=True, stop=True),
                    waits=[ev_exp[(NT - 1, 1)]])
            ec1 = vop(lambda E: E.tensor_copy(BCR[:, 0:1024], PA.ap()[:, :]), waits=[eb1])
            ec2 = vop(lambda E: E.tensor_copy(BCR[:, 1024:2048], PB.ap()[:, :]),
                      waits=[eb2])
            e = None
            for qh in range(2):
                e = vop(lambda E, qh=qh, c=c, b0=b0: E.tensor_tensor(
                    CXT[b0:b0 + 64, c * S + qh * 1024: c * S + (qh + 1) * 1024],
                    PC.ap()[0:64, ts(qh, 1024)], BCR[0:64, ts(qh, 1024)], OP.mult),
                    waits=[ec1, ec2])
            st["evac"][h] = e
            st["explast"][h] = ec2
            return e

        def attn_b_head(c, h2, h, ev_qp, ev_kp, ev_selq, ev_ok98, ev_v):
            b0 = h2 * 64
            evac_prev = st["evac"].get(h - 1)
            explast_prev = st["explast"].get(h - 1)
            e1 = vop(lambda E, b0=b0: E.tensor_reduce(
                SM2[0:64, 128:160], QP[b0:b0 + 64, :].rearrange("p (b n) -> p b n", n=64),
                AX.X, OP.add), waits=[ev_qp])
            vop(lambda E, b0=b0: E.tensor_reduce(
                SM2[0:64, 160:192], KP[b0:b0 + 64, :].rearrange("p (b n) -> p b n", n=64),
                AX.X, OP.add), waits=[ev_kp])
            e3 = vop(lambda E: E.tensor_copy(SMB[0:64, 0:32], SM2[0:64, 128:160]))
            e4 = vop(lambda E: E.tensor_copy(SMB[0:64, 32:64], SM2[0:64, 160:192]))
            ebs = top(lambda E: E.matmul(PA.ap()[0:32, 0:32], SMB[0:64, 0:32],
                                         SMB[0:64, 32:64], start=True, stop=True),
                      waits=[e3, e4, explast_prev, evac_prev])
            em1 = vop(lambda E: E.tensor_scalar(SM2[0:32, 192:224], PA.ap()[0:32, 0:32],
                                                BS_THR, None, OP.is_gt), waits=[ebs])
            vpad()
            vop(lambda E: E.tensor_scalar(SM2[0:32, 224:256], SM2[0:32, 192:224],
                                          1.0, XM, OP.subtract, OP.mult))
            vpad()
            eb1 = vop(lambda E: E.tensor_tensor(SM2[0:32, 224:256], SM2[0:32, 224:256],
                                                OMI32F, OP.mult), waits=[ev_cf])
            etr = top(lambda E: E.transpose(PA.ap()[0:32, 32:64], SM2[0:32, 224:256],
                                            ID128F[0:32, 0:32]), waits=[eb1])
            ebt = vop(lambda E: E.tensor_copy(SMB[0:32, 64:96], PA.ap()[0:32, 32:64]),
                      waits=[etr])
            exps = []
            prev_ev = None
            for kh in range(2):
                em = None
                for j in range(2):
                    em = top(lambda E, kh=kh, j=j: E.matmul(
                        PB.ap()[0:32, ts(j, 512)], SMB[0:32, 64:96],
                        SELK[:, kh * 1024 + j * 512: kh * 1024 + (j + 1) * 512],
                        start=True, stop=True), waits=[ebt, ev_consts, prev_ev])
                prev_ev = vop(lambda E, kh=kh: E.tensor_copy(
                    K98[64:96, ts(kh, 1024)], PB.ap()[0:32, 0:1024]),
                    waits=[em, ev_ok98, explast_prev])
                exps.append(prev_ev)
            vop(lambda E: E.tensor_tensor(SM2[0:32, 256:288], SM2[0:32, 192:224],
                                          I32F, OP.mult), waits=[em1])
            vpad()
            em1d = vop(lambda E: E.tensor_reduce(SM2[0:32, 288:289], SM2[0:32, 256:288],
                                                 AX.X, OP.add))
            vpad()
            ei = vop(lambda E: E.tensor_scalar(SM2[0:32, 289:290], SM2[0:32, 288:289],
                                               1.0, XM, OP.subtract, OP.mult))
            vpad()
            er_ = vop(lambda E: E.tensor_scalar(SMB[0:32, 96:112], E2C,
                                                SM2[0:32, 289:290], None, OP.mult),
                      waits=[ei])
            eim = top(lambda E: E.matmul(PA.ap()[:, 64:80], LC, SMB[0:32, 96:112],
                                         start=True, stop=True), waits=[er_])
            eimv = vop(lambda E: E.tensor_copy(SM2[:, 0:16], PA.ap()[:, 64:80]),
                       waits=[eim])
            esq = vop(lambda E, b0=b0: E.tensor_copy(Q98[0:64, :], QP[b0:b0 + 64, :]),
                      waits=[ev_selq, explast_prev])
            esk = vop(lambda E, b0=b0: E.tensor_copy(K98[0:64, :], KP[b0:b0 + 64, :]))
            ev_expA = {}
            for t in range(NT):
                for kh in range(2):
                    P = PA if kh == 0 else PB
                    dfix = (t // 8 == kh)
                    for j in range(2):
                        em = top(lambda E, t=t, kh=kh, P=P, dfix=dfix, j=j: E.matmul(
                            P.ap()[:, ts(j, 512)], Q98[0:96, ts(t, 128)],
                            K98[0:96, kh * 1024 + j * 512: kh * 1024 + (j + 1) * 512],
                            start=True, stop=not (dfix and (t * 128 - kh * 1024) // 512 == j)),
                            waits=[esq, esk, exps[kh], eimv, ev_expA.get((t - 1, kh))])
                    if dfix:
                        off = t * 128 - kh * 1024
                        edg = vop(lambda E, t=t: E.tensor_scalar(
                            DGR, ID128F, SM2[:, t:t + 1], None, OP.mult),
                            waits=[eimv, ev_expA.get((t - 1, kh))])
                        em = top(lambda E, off=off, P=P: E.matmul(
                            P.ap()[:, off:off + 128], DGR, EYE2,
                            start=False, stop=True), waits=[edg, ev_consts])
                    ev_expA[(t, kh)] = sop(lambda E, t=t, kh=kh, P=P: E.activation(
                        EA[:, ts(kh, 1024)], P.ap()[:, :], AF.Exp, bias=SHIFT,
                        scale=0.125, accum_out=SM2[:, 16 + kh:17 + kh]),
                        waits=[em, ev_cf, st["cnt"].get((h, t - 1))])
                esg = vop(lambda E, t=t: E.tensor_tensor(
                    SIG[:, t:t + 1], SM2[:, 16:17], SM2[:, 17:18], OP.add),
                    waits=[ev_expA[(t, 0)], ev_expA[(t, 1)]])
                vpad()
                vop(lambda E, t=t: E.tensor_scalar(SM2[:, 18:19], SIG[:, t:t + 1],
                                                   0.01, None, OP.mult))
                vpad()
                eI = vop(lambda E: E.tensor_scalar(EA[:, :], EA[:, :], SM2[:, 18:19],
                                                   None, OP.is_gt))
                for kh in range(2):
                    for j in range(2):
                        st["cnt"][(h, t)] = top(lambda E, t=t, kh=kh, j=j: E.matmul(
                            PC.ap()[0:32, kh * 1024 + j * 512: kh * 1024 + (j + 1) * 512],
                            SELT[:, ts(t, 32)],
                            EA[:, kh * 1024 + j * 512: kh * 1024 + (j + 1) * 512],
                            start=(t == 0), stop=(t == NT - 1)),
                            waits=[eI, ev_consts, evac_prev])
            eln = sop(lambda E: E.activation(SM2[:, 32:48], SIG[:, 0:16], AF.Ln),
                      waits=[vop(lambda E: E.tensor_copy(SM2[:, 19:20], SM2[:, 18:19]))])
            ev8 = vop(lambda E: E.tensor_scalar(SM2[:, 48:64], SM2[:, 32:48],
                                                -8.0, None, OP.mult), waits=[eln])
            vpad()
            vop(lambda E: E.tensor_copy(VHB, SM2[:, 48:64]))
            vpad()
            vop(lambda E: E.tensor_copy(SM2[:, 64:80], VHB))
            vpad()
            evlo = vop(lambda E: E.tensor_tensor(SM2[:, 80:96], SM2[:, 48:64],
                                                 SM2[:, 64:80], OP.subtract))
            etv = top(lambda E: E.transpose(PA.ap()[0:32, 96:224], SM2[:, 64:96], ID128F),
                      waits=[evlo, ev_expA[(NT - 1, 0)]])
            evv = vop(lambda E: E.tensor_copy(SMB[0:32, 416:544], PA.ap()[0:32, 96:224]),
                      waits=[etv])
            ed1 = sch.dma(G, scr[0:32, :], SMB[0:32, 416:544], waits=[evv])
            ed2 = sch.dma(G, Q98[96:97, :].rearrange("o (t p) -> o t p", p=128),
                          scr[0:16, :], waits=[ed1, explast_prev])
            ed3 = sch.dma(G, Q98[97:98, :].rearrange("o (t p) -> o t p", p=128),
                          scr[16:32, :], waits=[ed2])
            ecb = vop(lambda E: E.tensor_reduce(
                SM2[0:32, 96:128], PC.ap()[0:32, :].rearrange("p (b n) -> p b n", n=64),
                AX.X, OP.add), waits=[st["cnt"][(h, NT - 1)]])
            vpad()
            em2 = vop(lambda E: E.tensor_scalar(SM2[0:32, 128:160], SM2[0:32, 96:128],
                                                0.5, None, OP.is_gt), waits=[ecb])
            if DBG_STOP == "afterA":
                st["evac"][h] = em2
                st["explast"][h] = ev_expA[(NT - 1, 1)]
                st["vdone"][(h, NT - 1)] = st["cnt"][(h, NT - 1)]
                return em2
            vpad()
            vop(lambda E: E.tensor_tensor(SM2[0:32, 224:256], SM2[0:32, 192:224],
                                          SM2[0:32, 128:160], OP.add))
            vpad()
            vop(lambda E: E.tensor_scalar(SM2[0:32, 224:256], SM2[0:32, 224:256],
                                          2.0, XM, OP.subtract, OP.mult))
            vpad()
            ebt2 = vop(lambda E: E.tensor_tensor(SM2[0:32, 224:256], SM2[0:32, 224:256],
                                                 OMI32F, OP.mult))
            etr2 = top(lambda E: E.transpose(PA.ap()[0:32, 32:64], SM2[0:32, 224:256],
                                             ID128F[0:32, 0:32]),
                       waits=[ebt2, ev_expA[(NT - 1, 0)]])
            ebt3 = vop(lambda E: E.tensor_copy(SMB[0:32, 64:96], PA.ap()[0:32, 32:64]),
                       waits=[etr2])
            expsB = []
            prev_evB = None
            for kh in range(2):
                em = None
                for j in range(2):
                    em = top(lambda E, kh=kh, j=j: E.matmul(
                        PB.ap()[0:32, ts(j, 512)], SMB[0:32, 64:96],
                        SELK[:, kh * 1024 + j * 512: kh * 1024 + (j + 1) * 512],
                        start=True, stop=True),
                        waits=[ebt3, ev_expA[(NT - 1, 1)], prev_evB])
                prev_evB = vop(lambda E, kh=kh: E.tensor_copy(
                    K98[64:96, ts(kh, 1024)], PB.ap()[0:32, 0:1024]), waits=[em])
                expsB.append(prev_evB)
            vop(lambda E: E.tensor_tensor(SM2[0:32, 256:288], SM2[0:32, 128:160],
                                          I32F, OP.mult), waits=[em2])
            vpad()
            vop(lambda E: E.tensor_reduce(SM2[0:32, 290:291], SM2[0:32, 256:288],
                                          AX.X, OP.add))
            vpad()
            vop(lambda E: E.tensor_tensor(SM2[0:32, 291:292], SM2[0:32, 288:289],
                                          SM2[0:32, 290:291], OP.add))
            vpad()
            ei2 = vop(lambda E: E.tensor_scalar(SM2[0:32, 292:293], SM2[0:32, 291:292],
                                                2.0, XM, OP.subtract, OP.mult))
            vpad()
            er2 = vop(lambda E: E.tensor_scalar(SMB[0:32, 96:112], E2C,
                                                SM2[0:32, 292:293], None, OP.mult),
                      waits=[ei2])
            eim2 = top(lambda E: E.matmul(PA.ap()[:, 64:80], LC, SMB[0:32, 96:112],
                                          start=True, stop=True), waits=[er2])
            eimv2 = vop(lambda E: E.tensor_copy(SM2[:, 0:16], PA.ap()[:, 64:80]),
                        waits=[eim2])
            ev_expB = {}
            for u in range(NT):
                eb = u % 2
                for qh in range(2):
                    P = PA if qh == 0 else PB
                    dfix = (u // 8 == qh)
                    for j in range(2):
                        em = top(lambda E, u=u, qh=qh, P=P, dfix=dfix, j=j: E.matmul(
                            P.ap()[:, ts(j, 512)], K98[0:98, ts(u, 128)],
                            Q98[0:98, qh * 1024 + j * 512: qh * 1024 + (j + 1) * 512],
                            start=True, stop=not (dfix and (u * 128 - qh * 1024) // 512 == j)),
                            waits=[expsB[0], expsB[1], ed3, eimv2,
                                   ev_expB.get((u - 1, qh))])
                    if dfix:
                        off = u * 128 - qh * 1024
                        edg = vop(lambda E, u=u: E.tensor_scalar(
                            DGR, ID128F, SM2[:, u:u + 1], None, OP.mult),
                            waits=[eimv2, ev_expB.get((u - 1, qh))])
                        em = top(lambda E, off=off, P=P: E.matmul(
                            P.ap()[:, off:off + 128], DGR, EYE2,
                            start=False, stop=True), waits=[edg])
                    ev_expB[(u, qh)] = sop(lambda E, u=u, qh=qh, P=P, eb=eb: E.activation(
                        ET[:, eb * S + qh * 1024: eb * S + (qh + 1) * 1024],
                        P.ap()[:, :], AF.Exp, bias=SHIFT, scale=0.125),
                        waits=[em, st["vdone"].get((h, u - 2))])
                for qc in range(4):
                    st["vdone"][(h, u)] = top(lambda E, u=u, qc=qc, eb=eb, h=h: E.matmul(
                        PC.ap()[0:64, ts(qc, 512)],
                        VT[:, u * 1024 + h * 64: u * 1024 + (h + 1) * 64],
                        ET[:, eb * S + qc * 512: eb * S + (qc + 1) * 512],
                        start=(u == 0), stop=(u == NT - 1)),
                        waits=[ev_expB[(u, 0)], ev_expB[(u, 1)], ev_v, em2,
                               evac_prev])
            e = None
            for qh in range(2):
                e = vop(lambda E, qh=qh, c=c, b0=b0: E.tensor_copy(
                    CXT[b0:b0 + 64, c * S + qh * 1024: c * S + (qh + 1) * 1024],
                    PC.ap()[0:64, ts(qh, 1024)]), waits=[st["vdone"][(h, NT - 1)]])
            st["evac"][h] = e
            st["explast"][h] = ev_expB[(NT - 1, 1)]
            return e

        def emit_attention(kind, sq, ev_x):
            masks = (kind == "attn_b")
            wq, wk, wv, wo = (wdr["wq_" + kind[-1]], wdr["wk_" + kind[-1]],
                              wdr["wv_" + kind[-1]], wdr["wo_" + kind[-1]])
            st["vdone"] = {}
            st["evac"] = {}
            st["explast"] = {}
            st["cnt"] = {}
            ev_xnt = []
            ln_batch([ev_x])
            for t in range(NT):
                e = ln_norm_tile(t)
                ev_xnt.append(transpose_to(XNT, XN, t, e, t % 2,
                                           extra_waits=[ev_xnt[t - 2] if t >= 2 else None]))
            wv_evs = load_weight_full(wv, [ev_x])
            ev_vt = []
            for t in range(NT):
                P = PA if t % 2 == 0 else PB
                last = None
                for dc in range(NC):
                    for j in range(2):
                        last = top(lambda E, t=t, dc=dc, P=P, j=j: E.matmul(
                            P.ap()[:, ts(j, 512)],
                            XNT[:, dc * S + t * 128: dc * S + (t + 1) * 128],
                            WBF[:, dc * 1024 + j * 512: dc * 1024 + (j + 1) * 512],
                            start=(dc == 0), stop=(dc == NC - 1)),
                            waits=[ev_xnt[-1], wv_evs[dc],
                                   ev_vt[t - 2] if t >= 2 else None])
                ev_vt.append(vop(lambda E, t=t, P=P: E.tensor_copy(
                    VT[:, ts(t, 1024)], P.ap()[:, :]), waits=[last]))
            ev_v_all = ev_vt[-1]

            ev_selq = None
            ev_ok98 = None
            if masks:
                ev_selq = vop(lambda E: E.tensor_copy(Q98[64:96, :], SELK),
                              waits=[ev_consts])
                ev_ok98 = vop(lambda E: E.memset(K98[96:98, :], 1.0))
            ev_wprev = wv_evs[-1]
            last_head_ev = None
            for c in range(NC):
                eq = load_weight_colpair(wq, c, WBF2, [ev_wprev])
                ek = load_weight_colpair(wk, c, WBF3, [eq])
                ev_wprev = ek
                ev_qp = ev_kp = None
                prev_pair_ev = [st["evac"].get(2 * c - 1), st["explast"].get(2 * c - 1)]
                for which, wbf, dst, evw in (("q", WBF2, QP, eq), ("k", WBF3, KP, ek)):
                    e = None
                    for th in range(2):
                        P = PA if th == 0 else PB
                        last = None
                        for dc in range(NC):
                            for j in range(2):
                                last = top(lambda E, wbf=wbf, dc=dc, th=th, P=P, j=j:
                                           E.matmul(
                                    P.ap()[:, ts(j, 512)], wbf[:, ts(dc, 128)],
                                    XNT[:, dc * S + th * 1024 + j * 512:
                                        dc * S + th * 1024 + (j + 1) * 512],
                                    start=(dc == 0), stop=(dc == NC - 1)),
                                    waits=[evw, ev_vt[-1]] + prev_pair_ev)
                        e = vop(lambda E, dst=dst, th=th, P=P: E.tensor_copy(
                            dst[:, ts(th, 1024)], P.ap()[:, :]), waits=[last])
                    if which == "q":
                        ev_qp = e
                    else:
                        ev_kp = e
                for h2 in range(2):
                    h = 2 * c + h2
                    if masks:
                        last_head_ev = attn_b_head(c, h2, h, ev_qp, ev_kp,
                                                   ev_selq, ev_ok98, ev_v_all)
                    else:
                        last_head_ev = attn_a_head(c, h2, h, ev_qp, ev_kp, ev_v_all)
                    if DBG_HEADS is not None and h + 1 >= DBG_HEADS:
                        break
                if DBG_HEADS is not None and 2 * c + 2 >= DBG_HEADS:
                    break
            if DBG_STOP is not None:
                return last_head_ev
            wo_evs = load_weight_full(wo, [last_head_ev])
            ev_res = None
            prev_add = {}
            for t in range(NT):
                P = PA if t % 2 == 0 else PB
                last = None
                for dc in range(NC):
                    for j in range(2):
                        last = top(lambda E, t=t, dc=dc, P=P, j=j: E.matmul(
                            P.ap()[:, ts(j, 512)],
                            CXT[:, dc * S + t * 128: dc * S + (t + 1) * 128],
                            WBF[:, dc * 1024 + j * 512: dc * 1024 + (j + 1) * 512],
                            start=(dc == 0), stop=(dc == NC - 1)),
                            waits=[last_head_ev, wo_evs[dc],
                                   st["explast"].get(NH - 1),
                                   prev_add.get(t - 2)])
                ev_res = vop(lambda E, t=t, P=P: E.tensor_tensor(
                    X[:, ts(t, 1024)], X[:, ts(t, 1024)], P.ap()[:, :], OP.add),
                    waits=[last])
                prev_add[t] = ev_res
            return ev_res

        def emit_ffn(sq, ev_x, final):
            ev_fw = build_ffn_weights(ev_x)
            ev_q = []
            ln_batch([ev_x])
            for t in range(NT):
                ln_norm_tile(t)
                vop(lambda E: E.tensor_reduce(SMALL[:, 208:209], XN[:, :], AX.X, OP.max,
                                              apply_absolute_value=True))
                vpad()
                vop(lambda E: E.tensor_scalar(SMALL[:, 209:210], SMALL[:, 208:209],
                                              1e-12, 1.0 / 127.0, OP.max, OP.mult))
                vpad()
                vop(lambda E: E.reciprocal(SMALL[:, 210:211], SMALL[:, 209:210]))
                vop(lambda E, t=t: E.tensor_copy(LSA[:, t:t + 1], SMALL[:, 209:210]))
                vop(lambda E: E.tensor_scalar(XSC[:, :], XN[:, :],
                                              SMALL[:, 210:211], None, OP.mult))
                vop(lambda E: E.tensor_scalar(XNB[:, 0:1024], XSC[:, :], 0.0, 128.0,
                                              OP.max, OP.add))
                vop(lambda E: E.tensor_scalar(MSCR[:, :], XSC[:, :], 0.0, -128.0,
                                              OP.min, OP.add))
                vop(lambda E: E.tensor_tensor(XNB[:, 0:1024], XNB[:, 0:1024], MSCR[:, :], OP.add))
                vop(lambda E: E.tensor_scalar(XNB[:, 0:1024], XNB[:, 0:1024], -127.0, 127.0,
                                              OP.max, OP.min))
                vop(lambda E: E.tensor_copy(XI[:, :], XNB[:, 0:1024]))
                eqf = vop(lambda E: E.tensor_copy(XSC[:, :], XI[:, :]))
                ev_q.append(transpose_to(XNT, XSC, t, eqf, t % 2,
                                         extra_waits=[ev_q[t - 2] if t >= 2 else None]))
            ev_res = None
            prev_stt = {}
            prev_dma = None
            for t in range(NT):
                P = PA if t % 2 == 0 else PB
                last = None
                for dc in range(NC):
                    for j in range(2):
                        last = top(lambda E, t=t, dc=dc, P=P, j=j: E.matmul(
                            P.ap()[:, ts(j, 512)],
                            XNT[:, dc * S + t * 128: dc * S + (t + 1) * 128],
                            QRHS[:, dc * 1024 + j * 512: dc * 1024 + (j + 1) * 512],
                            start=(dc == 0), stop=(dc == NC - 1)),
                            waits=[ev_q[-1], ev_fw, prev_stt.get(t - 2)])
                e1 = vop(lambda E, t=t, P=P: E.scalar_tensor_tensor(
                    XSC[:, :], P.ap()[:, :], LSA[:, t:t + 1], RSB[:, :],
                    OP.mult, OP.mult), waits=[last])
                prev_stt[t] = e1
                if final:
                    ev_res = vop(lambda E, t=t: E.tensor_tensor(
                        XOUTS[:, :], X[:, ts(t, 1024)], XSC[:, :], OP.add),
                        waits=[prev_dma])
                    prev_dma = sch.dma(
                        G, xout[sq * S + t * 128: sq * S + (t + 1) * 128, :],
                        XOUTS[:, :], waits=[ev_res])
                else:
                    ev_res = vop(lambda E, t=t: E.tensor_tensor(
                        X[:, ts(t, 1024)], X[:, ts(t, 1024)], XSC[:, :], OP.add))
            return prev_dma if final else ev_res

        ev_x = None
        for sq in range(nseq):
            for t in range(NT):
                evd = sch.dma(SY, XSC[:, :],
                              xin[sq * S + t * 128: sq * S + (t + 1) * 128, :],
                              waits=[ev_x])
                ev_x = vop(lambda E, t=t: E.tensor_copy(X[:, ts(t, 1024)], XSC[:, :]),
                           waits=[evd])
            if stages[0].startswith("dbg"):
                mode = stages[0][4:]
                if mode == "x":
                    for t in range(NT):
                        e1 = vop(lambda E, t=t: E.tensor_copy(XOUTS[:, :],
                                                              X[:, ts(t, 1024)]),
                                 waits=[ev_x])
                        sch.dma(G, xout[sq * S + t * 128: sq * S + (t + 1) * 128, :],
                                XOUTS[:, :], waits=[e1])
                        ev_x = vop(lambda E: E.tensor_copy(SMALL[0:1, 150:151],
                                                           SMALL[0:1, 150:151]),
                                   waits=[("dma_sync", sch.ctr["dma_sync"])])
                elif mode == "ffnw":
                    efw = build_ffn_weights(ev_x)
                    prevd = None
                    for dc in range(NC):
                        e1 = vop(lambda E, dc=dc: E.tensor_copy(
                            XOUTS[:, :], QRHS[:, ts(dc, 1024)]), waits=[efw, prevd])
                        prevd = sch.dma(G, xout[dc * 128:(dc + 1) * 128, :],
                                        XOUTS[:, :], waits=[e1])
                    e1 = vop(lambda E: E.tensor_copy(XOUTS[:, :], RSB[:, :]),
                             waits=[efw, prevd])
                    prevd = sch.dma(G, xout[8 * 128:9 * 128, :], XOUTS[:, :], waits=[e1])
                    ev_x = prevd
                elif mode == "ln":
                    ln_batch([ev_x])
                    for t in range(NT):
                        e = ln_norm_tile(t, extra=[ev_x])
                        sch.dma(G, xout[sq * S + t * 128: sq * S + (t + 1) * 128, :],
                                XN[:, :], waits=[e])
                        ev_x = vop(lambda E: E.tensor_copy(DPAD[:, 0:1], DPAD[:, 2:3]),
                                   waits=[("dma_sync", sch.ctr["dma_sync"])])
                elif mode in ("vproj", "qp", "kp", "et", "ctx"):
                    ev2 = emit_attention("attn_a", sq, ev_x)
                    src_map = {"vproj": VT, "ctx": None}
                    if mode == "vproj":
                        for t in range(NT):
                            e1 = vop(lambda E, t=t: E.tensor_copy(
                                XOUTS[:, :], VT[:, ts(t, 1024)]), waits=[ev2])
                            sch.dma(G, xout[sq * S + t * 128: sq * S + (t + 1) * 128, :],
                                    XOUTS[:, :], waits=[e1])
                            ev2 = vop(lambda E: E.tensor_copy(SMALL[0:1, 150:151],
                                                              SMALL[0:1, 150:151]),
                                      waits=[("dma_sync", sch.ctr["dma_sync"])])
                    elif mode in ("qp", "kp"):
                        srcb = QP if mode == "qp" else KP
                        for j in range(2):
                            e1 = vop(lambda E, j=j, srcb=srcb: E.tensor_copy(
                                XOUTS[:, :], srcb[:, ts(j, 1024)]), waits=[ev2])
                            sch.dma(G, xout[j * 128:(j + 1) * 128, :], XOUTS[:, :],
                                    waits=[e1])
                            ev2 = vop(lambda E: E.tensor_copy(SMALL[0:1, 150:151],
                                                              SMALL[0:1, 150:151]),
                                      waits=[("dma_sync", sch.ctr["dma_sync"])])
                    elif mode == "et":
                        for j in range(2):
                            e1 = vop(lambda E, j=j: E.tensor_copy(
                                XOUTS[:, :], ET[:, ts(j, 1024)]), waits=[ev2])
                            sch.dma(G, xout[j * 128:(j + 1) * 128, :], XOUTS[:, :],
                                    waits=[e1])
                            ev2 = vop(lambda E: E.tensor_copy(SMALL[0:1, 150:151],
                                                              SMALL[0:1, 150:151]),
                                      waits=[("dma_sync", sch.ctr["dma_sync"])])
                    elif mode == "ctx":
                        for c in range(NC):
                            for j in range(2):
                                e1 = vop(lambda E, c=c, j=j: E.tensor_copy(
                                    XOUTS[:, :], CXT[:, c * S + j * 1024:
                                                     c * S + (j + 1) * 1024]),
                                    waits=[ev2])
                                sch.dma(G, xout[(c * 2 + j) * 128:
                                                (c * 2 + j + 1) * 128, :],
                                        XOUTS[:, :], waits=[e1])
                                ev2 = vop(lambda E: E.tensor_copy(
                                    SMALL[0:1, 150:151], SMALL[0:1, 150:151]),
                                    waits=[("dma_sync", sch.ctr["dma_sync"])])
                    ev_x = ev2
                continue
            for st_i, stage in enumerate(stages):
                final = (st_i == len(stages) - 1)
                if stage.startswith("attn"):
                    ev_x = emit_attention(stage, sq, ev_x)
                    if final:
                        prev_dma = None
                        for t in range(NT):
                            e1 = vop(lambda E, t=t: E.tensor_copy(
                                XOUTS[:, :], X[:, ts(t, 1024)]),
                                waits=[ev_x, prev_dma])
                            prev_dma = sch.dma(
                                G, xout[sq * S + t * 128: sq * S + (t + 1) * 128, :],
                                XOUTS[:, :], waits=[e1])
                        ev_x = prev_dma
                else:
                    ev_x = emit_ffn(sq, ev_x, final)

        sch.op(G, lambda E: E.memset(SMB[0:1, 400:404], 0.0),
               waits=[("dma_sync", sch.ctr["dma_sync"]), ("v", sch.ctr["v"]),
                      ("t", sch.ctr["t"]), ("s", sch.ctr["s"])])
        sch.align_dma_waits()
        sch.check()
        sch.emit(blk)
    return nc


_NC_CACHE = {}


def _get_program(stages, nseq):
    key = (tuple(stages), nseq)
    if key not in _NC_CACHE:
        _NC_CACHE[key] = build_program(stages, nseq)
    return _NC_CACHE[key]


def kernel(L0_attn_a_hidden_states, L0_attn_a_Wq, L0_attn_a_Wk, L0_attn_a_Wv,
           L0_attn_a_Wo, L0_attn_b_wq, L0_attn_b_wk, L0_attn_b_wv, L0_attn_b_wo,
           L0_ffn_rhs, _stages=("attn_a", "ffn", "attn_b", "ffn")):
    x = np.ascontiguousarray(np.asarray(L0_attn_a_hidden_states, dtype=np.float32))
    B = x.shape[0]
    xf = x.reshape(B * 2048, D)
    cf, cb = host_consts()
    wT = {
        "wq_a": np.ascontiguousarray(np.asarray(L0_attn_a_Wq, np.float32).T),
        "wk_a": np.ascontiguousarray(np.asarray(L0_attn_a_Wk, np.float32).T),
        "wv_a": np.ascontiguousarray(np.asarray(L0_attn_a_Wv, np.float32).T),
        "wo_a": np.ascontiguousarray(np.asarray(L0_attn_a_Wo, np.float32).T),
        "wq_b": np.ascontiguousarray(np.asarray(L0_attn_b_wq, np.float32).T),
        "wk_b": np.ascontiguousarray(np.asarray(L0_attn_b_wk, np.float32).T),
        "wv_b": np.ascontiguousarray(np.asarray(L0_attn_b_wv, np.float32).T),
        "wo_b": np.ascontiguousarray(np.asarray(L0_attn_b_wo, np.float32).T),
    }
    rhsT = np.ascontiguousarray(np.asarray(L0_ffn_rhs, np.float32).T)
    nseq = B // 8
    ncore = _get_program(_stages, nseq)
    in_maps = []
    for core in range(8):
        m = {"xin": np.ascontiguousarray(xf[core * nseq * S:(core + 1) * nseq * S, :]),
             "frhsT": rhsT, "cf": cf, "cb": cb}
        m.update(wT)
        in_maps.append(m)
    res = run_bass_kernel_spmd(ncore, in_maps, core_ids=list(range(8)))
    out = np.concatenate([np.asarray(r["xout"]) for r in res.results], axis=0)
    return out.reshape(B, 2048, D)


# version 04
# speedup vs baseline: 1.2470x; 1.2470x over previous
import numpy as np
import ml_dtypes
import concourse.bass as bass
import concourse.mybir as mybir
from concourse.alu_op_type import AluOpType
from concourse.bass_utils import run_bass_kernel_spmd

F32 = mybir.dt.float32
BF16 = mybir.dt.bfloat16
I32DT = mybir.dt.int32
AF = mybir.ActivationFunctionType
AX = mybir.AxisListType
OP = AluOpType
bfloat16 = ml_dtypes.bfloat16

S = 2048
D = 1024
DBG_HEADS = None
DBG_STOP = None
NT = 16
NC = 8
NH = 16
HD = 64
XM = 1024.0
SHIFT = -5.0
BS_THR = 0.01 * 64 * 64 * 8

ENGINES = ("sync", "tensor", "vector", "scalar", "gpsimd")


EPOCH = 3584


class Sched:
    def __init__(self):
        self.q = {e: [] for e in ENGINES}
        self.ctr = {"dma_sync": 0, "dma_gpsimd": 0, "t": 0, "s": 0, "v": 0}
        self.last_wait = {e: {} for e in ENGINES}
        self.sems = None

    def op(self, eng, fn, inc=None, waits=()):
        for ev in waits:
            if ev is None:
                continue
            semname, val = ev[0], ev[1]
            if val <= 0:
                continue
            if val > self.last_wait[eng].get(semname, 0):
                self.last_wait[eng][semname] = val
                self.q[eng].append(("wait", semname, val))
        ret = None
        if inc is not None:
            semname, amt = inc
            if (self.ctr[semname] % EPOCH) + amt > EPOCH:
                self.ctr[semname] = ((self.ctr[semname] + EPOCH - 1) // EPOCH) * EPOCH
            self.ctr[semname] += amt
            ret = (semname, self.ctr[semname], amt)
        self.q[eng].append(("op", fn, ret))
        return ret

    def dma(self, eng, out, in_, waits=()):
        eng = "sync"
        sem = "dma_" + eng
        return self.op(eng, lambda E, o=out, i=in_: E.dma_start(out=o, in_=i),
                       inc=(sem, 16), waits=waits)

    def align_dma_waits(self):
        run_end = {}
        v = 0
        ends = []
        runs = []
        cur = []
        for item in self.q["sync"]:
            if item[0] == "wait":
                if cur:
                    runs.append(cur)
                    cur = []
            elif item[2] is not None and item[2][0] == "dma_sync":
                cur.append(item[2][1])
        if cur:
            runs.append(cur)
        for r in runs:
            end = r[-1]
            for val in r:
                run_end[val] = end
        for e in ENGINES:
            newq = []
            maxw = {}
            for item in self.q[e]:
                if item[0] == "wait" and item[1] == "dma_sync":
                    val = run_end.get(item[2], item[2])
                    if val <= maxw.get("dma_sync", 0):
                        continue
                    maxw["dma_sync"] = val
                    newq.append(("wait", "dma_sync", val))
                else:
                    if item[0] == "wait":
                        if item[2] <= maxw.get(item[1], 0):
                            continue
                        maxw[item[1]] = item[2]
                    newq.append(item)
            self.q[e] = newq

    def check(self):
        pos = {e: 0 for e in ENGINES}
        val = {sn: 0 for sn in self.ctr}
        progress = True
        while progress:
            progress = False
            for e in ENGINES:
                while pos[e] < len(self.q[e]):
                    item = self.q[e][pos[e]]
                    if item[0] == "wait":
                        if val[item[1]] < item[2]:
                            break
                    else:
                        if item[2] is not None:
                            val[item[2][0]] = item[2][1]
                    pos[e] += 1
                    progress = True
        stuck = {e: (pos[e], len(self.q[e])) for e in ENGINES if pos[e] < len(self.q[e])}
        if stuck:
            detail = {e: self.q[e][pos[e]][:3] for e in stuck}
            raise RuntimeError(f"schedule deadlock: {stuck} blocked_on={detail}")

    def _phys(self, semname, val):
        k = (val - 1) // EPOCH
        return self.sems[semname][k], val - k * EPOCH

    def emit(self, block):
        for name, lst in self.sems.items():
            need = (self.ctr[name] + EPOCH - 1) // EPOCH
            assert need <= len(lst), (name, self.ctr[name], len(lst))
        for e in ENGINES:
            dec = getattr(block, e)

            def body(E, e=e):
                for item in self.q[e]:
                    if item[0] == "wait":
                        ps, lv = self._phys(item[1], item[2])
                        E.wait_ge(ps, lv)
                    else:
                        instr = item[1](E)
                        if item[2] is not None:
                            ps, _ = self._phys(item[2][0], item[2][1])
                            instr.then_inc(ps, item[2][2])
            dec(body)


def ts(i, n):
    return slice(i * n, (i + 1) * n)


import math as _math
SHIFT2 = SHIFT - _math.log(100.0)
SCALARS = [0.125, SHIFT, 0.01, 1.0 / D, 1e-5, 1.0 / 127.0, 1e-12, BS_THR,
           -8.0, 1.5, -0.5, 0.5, 128.0, -128.0, 127.0, -127.0, -1024.0, 1.0,
           2.0, 64.0, 1.0 / 64.0, -1.0, 0.0, SHIFT2]


def host_consts():
    cf = np.zeros((128, 256), dtype=np.float32)
    cf[:, 0:128] = np.eye(128, dtype=np.float32)
    cf[0:32, 128:160] = np.eye(32, dtype=np.float32)
    cf[0:32, 160:192] = 1.0 - np.eye(32, dtype=np.float32)
    for i, v in enumerate(SCALARS):
        cf[:, 192 + i] = v
    cb = np.zeros((128, 4096), dtype=np.float32)
    j = np.arange(2048)
    for c in range(32):
        cb[c, 0:2048] = (j // 64 == c).astype(np.float32)
    e64 = 1.0 - np.eye(64, dtype=np.float32)
    cb[0:64, 2048:2112] = e64
    cb[64:128, 2112:2176] = e64
    p = np.arange(128)
    for a in range(32):
        cb[a, 2176:2304] = ((p < 64) if a % 2 == 0 else (p >= 64)).astype(np.float32)
        cb[a, 2304:2320] = (np.arange(16) == a // 2).astype(np.float32)
    for t in range(16):
        for a in range(32):
            cb[:, 2320 + t * 32 + a] = (a == 2 * t + p // 64).astype(np.float32)
    return cf, cb


def build_program(stages=("attn_a", "ffn", "attn_b", "ffn"), nseq=2):
    nc = bass.Bass("TRN2", target_bir_lowering=False,
                   detect_race_conditions=False)
    xin = nc.dram_tensor("xin", [nseq * S, D], F32, kind="ExternalInput").ap()
    wdr = {}
    for nm in ("wq_a", "wk_a", "wv_a", "wo_a", "wq_b", "wk_b", "wv_b", "wo_b"):
        wdr[nm] = nc.dram_tensor(nm, [D, D], F32, kind="ExternalInput").ap()
    frhsT = nc.dram_tensor("frhsT", [D, D], F32, kind="ExternalInput").ap()
    cfD = nc.dram_tensor("cf", [128, 256], F32, kind="ExternalInput").ap()
    cbD = nc.dram_tensor("cb", [128, 4096], F32, kind="ExternalInput").ap()
    xout = nc.dram_tensor("xout", [nseq * S, D], F32, kind="ExternalOutput").ap()
    scr = nc.dram_tensor("scr_rows", [32, 128], BF16, kind="Internal").ap()

    sch = Sched()
    import contextlib
    est = contextlib.ExitStack()
    with est:
        sb = lambda name, shape, dt: est.enter_context(nc.sbuf_tensor(name, shape, dt))
        CF = sb("CF", [128, 216], F32)
        CBT = sb("CBT", [128, 2832], BF16)
        X = sb("X", [128, NT * D], BF16)
        XNT = sb("XNT", [128, NC * S], BF16)
        VT = sb("VT", [128, NT * 1040], BF16)
        CXT = sb("CXT", [128, NC * S], BF16)
        QP = sb("QP", [128, S], BF16)
        KP = sb("KP", [128, S], BF16)
        WBF = sb("WBF", [128, NC * D], BF16)
        WST2 = sb("WST2", [128, D], F32)
        WBF2 = sb("WBF2", [128, D], BF16)
        WBF3 = sb("WBF3", [128, D], BF16)
        ET = sb("ET", [128, S], BF16)
        EA = sb("EA", [128, S], BF16)
        Q98 = sb("Q98", [98, S], BF16)
        K98 = sb("K98", [98, S], BF16)
        SIG = sb("SIG", [128, NT], F32)
        LSA = sb("LSA", [128, NT], F32)
        SMALL = sb("SMALL", [128, 224], F32)
        DPAD = sb("DPAD", [128, 4], F32)
        SM2 = sb("SM2", [128, 296], F32)
        SMB = sb("SMB", [128, 544], BF16)
        XN = sb("XN", [128, D], F32)
        XSC = sb("XSC", [128, D], F32)
        XI = sb("XI", [128, D], I32DT)
        XNB = sb("XNB", [128, 2048], BF16)
        MSCR = ET[:, 0:1024]
        QRHS = WBF
        BCR = EA[:, 0:2048]
        EAB = WBF[:, 0:2048]
        RROW1 = XNB[0:1, 0:1024]
        RROW2 = XNB[0:1, 1024:2048]
        XOUTS = XN
        RSB = sb("RSB", [128, D], BF16)
        PA = est.enter_context(nc.psum_tensor("PA", [128, 1024], F32))
        PB = est.enter_context(nc.psum_tensor("PB", [128, 1024], F32))
        PC = est.enter_context(nc.psum_tensor("PC", [128, 2048], F32))

        sems = {}
        for sn, cnt in (("dma_sync", 10), ("dma_gpsimd", 1), ("t", 10), ("s", 4),
                        ("v", 16)):
            sems[sn] = [est.enter_context(nc.semaphore(f"{sn}_{i}"))
                        for i in range(cnt)]
        sch.sems = sems

        for i, v in enumerate(SCALARS):
            nc.const_aps.register(F32, float(v), CF[:, 192 + i:193 + i])

        ID128F = CF[:, 0:128]
        I32F = CF[0:32, 128:160]
        OMI32F = CF[0:32, 160:192]
        SELK = CBT[0:32, 0:2048]
        EYE2 = CBT[0:128, 2048:2176]
        LC = CBT[0:32, 2176:2304]
        E2C = CBT[0:32, 2304:2320]
        SELT = CBT[0:128, 2320:2832]
        ONESC = SMB[:, 112:113]
        ONESR = SMB[0:1, 113:241]
        DGR = SMB[:, 256:384]
        VHB = SMB[:, 384:400]

        blk = est.enter_context(nc.Block())
        SY, T, V, SE, G = "sync", "tensor", "vector", "scalar", "gpsimd"

        def vop(fn, waits=()):
            return sch.op(V, fn, inc=("v", 1), waits=waits)

        def top(fn, waits=()):
            return sch.op(T, fn, inc=("t", 1), waits=waits)

        def sop(fn, waits=()):
            return sch.op(SE, fn, inc=("s", 1), waits=waits)

        def vpad():
            sch.op(V, lambda E: E.tensor_copy(DPAD[:, 0:2], DPAD[:, 2:4]), inc=None)

        ev_cf = sch.dma(SY, CF[:, :], cfD[:, 0:216])
        ev_cb = None
        for q4 in range(3):
            w = 1024 if q4 < 2 else 2832 - 2048
            evd = sch.dma(SY, WST2[:, 0:w], cbD[:, q4 * 1024:q4 * 1024 + w],
                          waits=[ev_cb])
            ev_cb = vop(lambda E, q4=q4, w=w: E.tensor_copy(
                CBT[:, q4 * 1024:q4 * 1024 + w], WST2[:, 0:w]), waits=[evd])
        ev_consts = ev_cb
        vop(lambda E: E.memset(DPAD[:, :], 0.0))
        ev_ones = vop(lambda E: E.memset(SMB[:, 112:113], 1.0))
        ev_ones2 = vop(lambda E: E.memset(SMB[0:1, 113:241], 1.0))
        ev_vones = vop(lambda E: E.memset(
            VT.ap().rearrange("p (u q) -> p u q", q=65)[:, :, 64:65], 1.0))

        def build_ffn_weights(ev_enter):
            ev_prev = [e for e in (ev_consts, ev_enter) if e is not None][-1]
            ev_evac = {}
            for fc in range(NC):
                evd = sch.dma(SY, WST2[:, :], frhsT[ts(fc, 128), :], waits=[ev_prev])
                vop(lambda E, fc=fc: E.tensor_reduce(
                    SMALL[:, fc:fc + 1], WST2[:, :], AX.X, OP.max,
                    apply_absolute_value=True), waits=[evd])
                vpad()
                vop(lambda E, fc=fc: E.tensor_scalar(
                    SMALL[:, 16 + fc:17 + fc], SMALL[:, fc:fc + 1], 1e-12, 1.0 / 127.0,
                    OP.max, OP.mult))
                vpad()
                vop(lambda E, fc=fc: E.reciprocal(
                    SMALL[:, 32 + fc:33 + fc], SMALL[:, 16 + fc:17 + fc]))
                vpad()
                ev_prev = vop(lambda E, fc=fc: E.tensor_scalar(
                    XSC[:, :], WST2[:, :], SMALL[:, 32 + fc:33 + fc], None, OP.mult))
                vop(lambda E: E.tensor_scalar(XNB[:, 0:1024], XSC[:, :], 0.0, 128.0,
                                              OP.max, OP.add))
                vop(lambda E: E.tensor_scalar(MSCR[:, :], XSC[:, :], 0.0, -128.0,
                                              OP.min, OP.add))
                vop(lambda E: E.tensor_tensor(XNB[:, 0:1024], XNB[:, 0:1024], MSCR[:, :], OP.add))
                vop(lambda E: E.tensor_scalar(XNB[:, 0:1024], XNB[:, 0:1024], -127.0, 127.0,
                                              OP.max, OP.min))
                vop(lambda E: E.tensor_copy(XI[:, :], XNB[:, 0:1024]))
                e = vop(lambda E: E.tensor_copy(XSC[:, :], XI[:, :]))
                last_t = None
                for dc in range(NC):
                    last_t = top(lambda E, fc=fc, dc=dc: E.transpose(
                        (PA if fc % 2 == 0 else PB).ap()[:, ts(dc, 128)],
                        XSC[:, ts(dc, 128)], ID128F),
                        waits=[e, ev_cf, ev_evac.get(fc - 2), ev_enter])
                ev_evac[fc] = vop(lambda E, fc=fc: E.tensor_copy(
                    QRHS.ap().rearrange("p (c n) -> p c n", n=D)[:, :, ts(0, 1) if False else slice(fc * 128, (fc + 1) * 128)],
                    (PA if fc % 2 == 0 else PB).ap().rearrange("p (c n) -> p c n", n=128)),
                    waits=[last_t])
            e = top(lambda E: E.transpose(PC.ap()[0:8, 0:128], SMALL[:, 16:24], ID128F),
                    waits=[ev_evac[NC - 1], ev_enter])
            e = vop(lambda E: E.tensor_copy(SMB[0:8, 256:384], PC.ap()[0:8, 0:128]),
                    waits=[e])
            evd = sch.dma(G, scr[0:8, :], SMB[0:8, 256:384], waits=[e])
            evd = sch.dma(G, RROW1.rearrange("o (c p) -> o c p", p=128),
                          scr[0:8, :], waits=[evd])
            em1 = top(lambda E: E.matmul(PA.ap()[:, 0:512], ONESR, RROW1[:, 0:512],
                                         start=True, stop=True),
                      waits=[evd, ev_ones2, ev_evac[NC - 1], ev_evac[NC - 2]])
            em2 = top(lambda E: E.matmul(PB.ap()[:, 0:512], ONESR, RROW1[:, 512:1024],
                                         start=True, stop=True))
            vop(lambda E: E.tensor_copy(RSB[:, 0:512], PA.ap()[:, 0:512]), waits=[em1])
            e2 = vop(lambda E: E.tensor_copy(RSB[:, 512:1024], PB.ap()[:, 0:512]),
                     waits=[em2])
            return e2

        st = {}

        def ln_batch(waits):
            for t in range(NT):
                vop(lambda E, t=t: E.tensor_reduce(SMALL[:, t:t + 1],
                                                   X[:, ts(t, 1024)], AX.X, OP.add),
                    waits=waits if t == 0 else ())
            for t in range(NT):
                vop(lambda E, t=t: E.tensor_tensor(XNB[:, 0:1024], X[:, ts(t, 1024)],
                                                   X[:, ts(t, 1024)], OP.mult))
                vop(lambda E, t=t: E.tensor_reduce(SMALL[:, 16 + t:17 + t],
                                                   XNB[:, 0:1024], AX.X, OP.add))
            vop(lambda E: E.tensor_scalar(SMALL[:, 32:48], SMALL[:, 0:16],
                                          1.0 / D, None, OP.mult))
            vop(lambda E: E.tensor_scalar(SMALL[:, 48:64], SMALL[:, 16:32],
                                          1.0 / D, None, OP.mult))
            vop(lambda E: E.tensor_tensor(SMALL[:, 64:80], SMALL[:, 32:48],
                                          SMALL[:, 32:48], OP.mult))
            vpad()
            vop(lambda E: E.tensor_tensor(SMALL[:, 80:96], SMALL[:, 48:64],
                                          SMALL[:, 64:80], OP.subtract))
            vpad()
            vop(lambda E: E.tensor_scalar(SMALL[:, 96:112], SMALL[:, 80:96],
                                          1e-5, None, OP.add))
            vpad()
            e = vop(lambda E: E.reciprocal(SMALL[:, 112:128], SMALL[:, 96:112]))
            eg = sop(lambda E: E.activation(SMALL[:, 128:144], SMALL[:, 112:128],
                                            AF.Sqrt), waits=[e, ev_cf])
            vop(lambda E: E.tensor_tensor(SMALL[:, 144:160], SMALL[:, 128:144],
                                          SMALL[:, 128:144], OP.mult), waits=[eg])
            vpad()
            vop(lambda E: E.tensor_tensor(SMALL[:, 160:176], SMALL[:, 96:112],
                                          SMALL[:, 144:160], OP.mult))
            vpad()
            vop(lambda E: E.tensor_scalar(SMALL[:, 176:192], SMALL[:, 160:176],
                                          -0.5, 1.5, OP.mult, OP.add))
            vpad()
            e = vop(lambda E: E.tensor_tensor(SMALL[:, 192:208], SMALL[:, 128:144],
                                              SMALL[:, 176:192], OP.mult))
            return e

        def ln_norm_tile(t, extra=()):
            e = vop(lambda E, t=t: E.tensor_scalar(XN[:, :], X[:, ts(t, 1024)],
                                                   SMALL[:, 32 + t:33 + t],
                                                   SMALL[:, 192 + t:193 + t],
                                                   OP.subtract, OP.mult), waits=extra)
            return e

        def transpose_to(dst, src, t, ev_in, pp, extra_waits=()):
            P = PA if pp == 0 else PB
            last = None
            for dc in range(NC):
                last = top(lambda E, dc=dc, P=P, src=src: E.transpose(
                    P.ap()[:, ts(dc, 128)], src[:, ts(dc, 128)], ID128F),
                    waits=[ev_in, ev_cf] + list(extra_waits))
            e = vop(lambda E, t=t, P=P, dst=dst: E.tensor_copy(
                dst.ap().rearrange("p (c n) -> p c n", n=S)[:, :, ts(t, 128)],
                P.ap().rearrange("p (c n) -> p c n", n=128)), waits=[last])
            return e

        def load_weight_full(wap, waits):
            evs = []
            last = None
            for dc in range(NC):
                evd = sch.dma(SY, WST2[:, :], wap[ts(dc, 128), :],
                              waits=list(waits) + [last])
                last = vop(lambda E, dc=dc: E.tensor_copy(WBF[:, ts(dc, 1024)],
                                                          WST2[:, :]), waits=[evd])
                evs.append(last)
            return evs

        def load_weight_colpair(wap, c, dstbf, waits):
            evd = sch.dma(SY, WST2[:, :].rearrange("p (c8 n) -> p c8 n", n=128),
                          wap[:, ts(c, 128)].rearrange("(c8 p) n -> p c8 n", p=128),
                          waits=waits)
            e = vop(lambda E, dstbf=dstbf: E.tensor_copy(dstbf[:, :], WST2[:, :]),
                    waits=[evd])
            return e

        def attn_a_head(c, h2, h, ev_qp, ev_kp, ev_v):
            b0 = h2 * 64
            vbase = h * 65
            ev_exp = {}
            ev_av = {}

            def emit_av(s):
                u, th = divmod(s, 2)
                e = None
                for j in range(2):
                    e = top(lambda E, u=u, th=th, s=s, j=j: E.matmul(
                        PC.ap()[0:65, th * 1024 + j * 512:
                                th * 1024 + (j + 1) * 512],
                        VT[:, u * 1040 + vbase: u * 1040 + vbase + 65],
                        ET[:, (s % 2) * 1024 + j * 512:
                           (s % 2) * 1024 + (j + 1) * 512],
                        start=(u == 0), stop=(u == NT - 1)),
                        waits=[ev_exp[s], ev_v, ev_vones, st["evac"].get(h - 1)])
                ev_av[s] = e

            for s in range(2 * NT):
                u, th = divmod(s, 2)
                P = PA if s % 2 == 0 else PB
                em = None
                for j in range(2):
                    em = top(lambda E, u=u, th=th, P=P, b0=b0, j=j: E.matmul(
                        P.ap()[:, ts(j, 512)], KP[b0:b0 + 64, ts(u, 128)],
                        QP[b0:b0 + 64, th * 1024 + j * 512:
                           th * 1024 + (j + 1) * 512], start=True, stop=True),
                        waits=[ev_qp, ev_kp, ev_exp.get(s - 2),
                               st["explast"].get(h - 1)])
                if s >= 1:
                    emit_av(s - 1)
                ev_exp[s] = sop(lambda E, P=P, s=s: E.activation(
                    ET[:, (s % 2) * 1024: (s % 2 + 1) * 1024],
                    P.ap()[:, :], AF.Exp, bias=SHIFT, scale=0.125),
                    waits=[em, ev_cf, ev_av.get(s - 2)])
            emit_av(2 * NT - 1)

            def _recip(E, dst, srcap):
                with nc.allow_low_precision(reason="bf16 recip feeds bf16 bcast matmul"):
                    return E.reciprocal(dst, srcap)
            vop(lambda E: _recip(E, RROW1, PC.ap()[64:65, 0:1024]),
                waits=[ev_av[2 * NT - 2]])
            er = vop(lambda E: _recip(E, RROW2, PC.ap()[64:65, 1024:2048]),
                     waits=[ev_av[2 * NT - 1]])
            for j in range(2):
                eb1 = top(lambda E, j=j: E.matmul(
                    PA.ap()[:, ts(j, 512)], ONESR, RROW1[:, ts(j, 512)],
                    start=True, stop=True),
                    waits=[er, ev_exp[2 * NT - 2], ev_ones2])
            for j in range(2):
                eb2 = top(lambda E, j=j: E.matmul(
                    PB.ap()[:, ts(j, 512)], ONESR, RROW2[:, ts(j, 512)],
                    start=True, stop=True),
                    waits=[ev_exp[2 * NT - 1]])
            ec1 = vop(lambda E: E.tensor_copy(BCR[:, 0:1024], PA.ap()[:, :]), waits=[eb1])
            ec2 = vop(lambda E: E.tensor_copy(BCR[:, 1024:2048], PB.ap()[:, :]),
                      waits=[ec1, eb2])
            e = None
            for qh in range(2):
                e = vop(lambda E, qh=qh, c=c, b0=b0: E.tensor_tensor(
                    CXT[b0:b0 + 64, c * S + qh * 1024: c * S + (qh + 1) * 1024],
                    PC.ap()[0:64, ts(qh, 1024)], BCR[0:64, ts(qh, 1024)], OP.mult),
                    waits=[ec2])
            st["evac"][h] = e
            st["explast"][h] = ec2
            return e

        def attn_b_head(c, h2, h, ev_qp, ev_kp, ev_selq, ev_ok98, ev_v):
            b0 = h2 * 64
            evac_prev = st["evac"].get(h - 1)
            explast_prev = st["explast"].get(h - 1)
            e1 = vop(lambda E, b0=b0: E.tensor_reduce(
                SM2[0:64, 128:160], QP[b0:b0 + 64, :].rearrange("p (b n) -> p b n", n=64),
                AX.X, OP.add), waits=[ev_qp])
            vop(lambda E, b0=b0: E.tensor_reduce(
                SM2[0:64, 160:192], KP[b0:b0 + 64, :].rearrange("p (b n) -> p b n", n=64),
                AX.X, OP.add), waits=[ev_kp])
            e3 = vop(lambda E: E.tensor_copy(SMB[0:64, 0:32], SM2[0:64, 128:160]))
            e4 = vop(lambda E: E.tensor_copy(SMB[0:64, 32:64], SM2[0:64, 160:192]))
            ebs = top(lambda E: E.matmul(PA.ap()[0:32, 0:32], SMB[0:64, 0:32],
                                         SMB[0:64, 32:64], start=True, stop=True),
                      waits=[e3, e4, explast_prev, evac_prev])
            em1 = vop(lambda E: E.tensor_scalar(SM2[0:32, 192:224], PA.ap()[0:32, 0:32],
                                                BS_THR, None, OP.is_gt), waits=[ebs])
            vpad()
            vop(lambda E: E.tensor_scalar(SM2[0:32, 224:256], SM2[0:32, 192:224],
                                          1.0, XM, OP.subtract, OP.mult))
            vpad()
            eb1 = vop(lambda E: E.tensor_tensor(SM2[0:32, 224:256], SM2[0:32, 224:256],
                                                OMI32F, OP.mult), waits=[ev_cf])
            etr = top(lambda E: E.transpose(PA.ap()[0:32, 32:64], SM2[0:32, 224:256],
                                            ID128F[0:32, 0:32]), waits=[eb1])
            ebt = vop(lambda E: E.tensor_copy(SMB[0:32, 64:96], PA.ap()[0:32, 32:64]),
                      waits=[etr])
            exps = []
            prev_ev = None
            for kh in range(2):
                em = None
                for j in range(2):
                    em = top(lambda E, kh=kh, j=j: E.matmul(
                        PB.ap()[0:32, ts(j, 512)], SMB[0:32, 64:96],
                        SELK[:, kh * 1024 + j * 512: kh * 1024 + (j + 1) * 512],
                        start=True, stop=True), waits=[ebt, ev_consts, prev_ev])
                prev_ev = vop(lambda E, kh=kh: E.tensor_copy(
                    K98[64:96, ts(kh, 1024)], PB.ap()[0:32, 0:1024]),
                    waits=[em, ev_ok98, explast_prev])
                exps.append(prev_ev)
            vop(lambda E: E.tensor_tensor(SM2[0:32, 256:288], SM2[0:32, 192:224],
                                          I32F, OP.mult), waits=[em1])
            vpad()
            em1d = vop(lambda E: E.tensor_reduce(SM2[0:32, 288:289], SM2[0:32, 256:288],
                                                 AX.X, OP.add))
            vpad()
            ei = vop(lambda E: E.tensor_scalar(SM2[0:32, 289:290], SM2[0:32, 288:289],
                                               1.0, XM, OP.subtract, OP.mult))
            vpad()
            er_ = vop(lambda E: E.tensor_scalar(SMB[0:32, 96:112], E2C,
                                                SM2[0:32, 289:290], None, OP.mult),
                      waits=[ei])
            eim = top(lambda E: E.matmul(PA.ap()[:, 64:80], LC, SMB[0:32, 96:112],
                                         start=True, stop=True), waits=[er_])
            eimv = vop(lambda E: E.tensor_copy(SM2[:, 0:16], PA.ap()[:, 64:80]),
                       waits=[eim])
            esq = vop(lambda E, b0=b0: E.tensor_copy(Q98[0:64, :], QP[b0:b0 + 64, :]),
                      waits=[ev_selq, explast_prev])
            esk = vop(lambda E, b0=b0: E.tensor_copy(K98[0:64, :], KP[b0:b0 + 64, :]))
            C001 = 192 + SCALARS.index(0.01)
            edg = vop(lambda E: E.tensor_scalar(
                DGR, ID128F, SM2[:, 0:1], None, OP.mult), waits=[eimv])
            ev_expA = {}
            for t in range(NT):
                EAt = EA if t % 2 == 0 else EAB
                for kh in range(2):
                    P = PA if kh == 0 else PB
                    dfix = (t // 8 == kh)
                    for j in range(2):
                        em = top(lambda E, t=t, kh=kh, P=P, dfix=dfix, j=j: E.matmul(
                            P.ap()[:, ts(j, 512)], Q98[0:96, ts(t, 128)],
                            K98[0:96, kh * 1024 + j * 512: kh * 1024 + (j + 1) * 512],
                            start=True, stop=not (dfix and (t * 128 - kh * 1024) // 512 == j)),
                            waits=[esq, esk, exps[kh], eimv, ev_expA.get((t - 1, kh))])
                    if dfix:
                        off = t * 128 - kh * 1024
                        em = top(lambda E, off=off, P=P: E.matmul(
                            P.ap()[:, off:off + 128], DGR, EYE2,
                            start=False, stop=True), waits=[edg, ev_consts])
                    ev_expA[(t, kh)] = sop(lambda E, t=t, kh=kh, P=P, EAt=EAt:
                                           E.activation(
                        EAt[:, ts(kh, 1024)], P.ap()[:, :], AF.Exp, bias=SHIFT,
                        scale=0.125, accum_out=SM2[:, 16 + kh:17 + kh]),
                        waits=[em, ev_cf, st["cnt"].get((h, t - 2))])
                esg = vop(lambda E, t=t: E.scalar_tensor_tensor(
                    SIG[:, t:t + 1], SM2[:, 16:17], SM2[:, 17:18],
                    CF[:, C001:C001 + 1], OP.add, OP.mult),
                    waits=[ev_expA[(t, 0)], ev_expA[(t, 1)]])
                if t + 1 < NT:
                    edg = vop(lambda E, t=t: E.tensor_scalar(
                        DGR, ID128F, SM2[:, t + 1:t + 2], None, OP.mult))
                else:
                    vpad()
                eI = vop(lambda E, t=t, EAt=EAt: E.tensor_scalar(
                    EAt[:, :], EAt[:, :], SIG[:, t:t + 1], None, OP.is_gt))
                for kh in range(2):
                    for j in range(2):
                        st["cnt"][(h, t)] = top(lambda E, t=t, kh=kh, j=j, EAt=EAt:
                                                E.matmul(
                            PC.ap()[0:32, kh * 1024 + j * 512: kh * 1024 + (j + 1) * 512],
                            SELT[:, ts(t, 32)],
                            EAt[:, kh * 1024 + j * 512: kh * 1024 + (j + 1) * 512],
                            start=(t == 0), stop=(t == NT - 1)),
                            waits=[eI, ev_consts, evac_prev])
            eln = sop(lambda E: E.activation(SM2[:, 32:48], SIG[:, 0:16], AF.Ln),
                      waits=[vop(lambda E: E.tensor_copy(DPAD[:, 0:2], DPAD[:, 2:4]))])
            ev8 = vop(lambda E: E.tensor_scalar(SM2[:, 48:64], SM2[:, 32:48],
                                                -8.0, None, OP.mult), waits=[eln])
            vpad()
            vop(lambda E: E.tensor_copy(VHB, SM2[:, 48:64]))
            vpad()
            vop(lambda E: E.tensor_copy(SM2[:, 64:80], VHB))
            vpad()
            evlo = vop(lambda E: E.tensor_tensor(SM2[:, 80:96], SM2[:, 48:64],
                                                 SM2[:, 64:80], OP.subtract))
            etv = top(lambda E: E.transpose(PA.ap()[0:32, 96:224], SM2[:, 64:96], ID128F),
                      waits=[evlo, ev_expA[(NT - 1, 0)]])
            evv = vop(lambda E: E.tensor_copy(SMB[0:32, 416:544], PA.ap()[0:32, 96:224]),
                      waits=[etv])
            ed1 = sch.dma(G, scr[0:32, :], SMB[0:32, 416:544], waits=[evv])
            ed2 = sch.dma(G, Q98[96:97, :].rearrange("o (t p) -> o t p", p=128),
                          scr[0:16, :], waits=[ed1, explast_prev])
            ed3 = sch.dma(G, Q98[97:98, :].rearrange("o (t p) -> o t p", p=128),
                          scr[16:32, :], waits=[ed2])
            ecb = vop(lambda E: E.tensor_reduce(
                SM2[0:32, 96:128], PC.ap()[0:32, :].rearrange("p (b n) -> p b n", n=64),
                AX.X, OP.add), waits=[st["cnt"][(h, NT - 1)]])
            vpad()
            em2 = vop(lambda E: E.tensor_scalar(SM2[0:32, 128:160], SM2[0:32, 96:128],
                                                0.5, None, OP.is_gt), waits=[ecb])
            if DBG_STOP == "afterA":
                st["evac"][h] = em2
                st["explast"][h] = ev_expA[(NT - 1, 1)]
                st["vdone"][(h, NT - 1)] = st["cnt"][(h, NT - 1)]
                return em2
            vpad()
            vop(lambda E: E.tensor_tensor(SM2[0:32, 224:256], SM2[0:32, 192:224],
                                          SM2[0:32, 128:160], OP.add))
            vpad()
            vop(lambda E: E.tensor_scalar(SM2[0:32, 224:256], SM2[0:32, 224:256],
                                          2.0, XM, OP.subtract, OP.mult))
            vpad()
            ebt2 = vop(lambda E: E.tensor_tensor(SM2[0:32, 224:256], SM2[0:32, 224:256],
                                                 OMI32F, OP.mult))
            etr2 = top(lambda E: E.transpose(PA.ap()[0:32, 32:64], SM2[0:32, 224:256],
                                             ID128F[0:32, 0:32]),
                       waits=[ebt2, ev_expA[(NT - 1, 0)]])
            ebt3 = vop(lambda E: E.tensor_copy(SMB[0:32, 64:96], PA.ap()[0:32, 32:64]),
                       waits=[etr2])
            expsB = []
            prev_evB = None
            for kh in range(2):
                em = None
                for j in range(2):
                    em = top(lambda E, kh=kh, j=j: E.matmul(
                        PB.ap()[0:32, ts(j, 512)], SMB[0:32, 64:96],
                        SELK[:, kh * 1024 + j * 512: kh * 1024 + (j + 1) * 512],
                        start=True, stop=True),
                        waits=[ebt3, ev_expA[(NT - 1, 1)], prev_evB])
                prev_evB = vop(lambda E, kh=kh: E.tensor_copy(
                    K98[64:96, ts(kh, 1024)], PB.ap()[0:32, 0:1024]), waits=[em])
                expsB.append(prev_evB)
            vop(lambda E: E.tensor_tensor(SM2[0:32, 256:288], SM2[0:32, 128:160],
                                          I32F, OP.mult), waits=[em2])
            vpad()
            vop(lambda E: E.tensor_reduce(SM2[0:32, 290:291], SM2[0:32, 256:288],
                                          AX.X, OP.add))
            vpad()
            vop(lambda E: E.tensor_tensor(SM2[0:32, 291:292], SM2[0:32, 288:289],
                                          SM2[0:32, 290:291], OP.add))
            vpad()
            ei2 = vop(lambda E: E.tensor_scalar(SM2[0:32, 292:293], SM2[0:32, 291:292],
                                                2.0, XM, OP.subtract, OP.mult))
            vpad()
            er2 = vop(lambda E: E.tensor_scalar(SMB[0:32, 96:112], E2C,
                                                SM2[0:32, 292:293], None, OP.mult),
                      waits=[ei2])
            eim2 = top(lambda E: E.matmul(PA.ap()[:, 64:80], LC, SMB[0:32, 96:112],
                                          start=True, stop=True), waits=[er2])
            eimv2 = vop(lambda E: E.tensor_copy(SM2[:, 0:16], PA.ap()[:, 64:80]),
                        waits=[eim2])
            ev_expB = {}
            ev_avB = {}

            def emit_avB(s):
                u, th = divmod(s, 2)
                e = None
                for j in range(2):
                    e = top(lambda E, u=u, th=th, s=s, j=j: E.matmul(
                        PC.ap()[0:64, th * 1024 + j * 512:
                                th * 1024 + (j + 1) * 512],
                        VT[:, u * 1040 + h * 65: u * 1040 + h * 65 + 64],
                        ET[:, (s % 2) * 1024 + j * 512:
                           (s % 2) * 1024 + (j + 1) * 512],
                        start=(u == 0), stop=(u == NT - 1)),
                        waits=[ev_expB[s], ev_v, em2, evac_prev])
                ev_avB[s] = e

            for s in range(2 * NT):
                u, th = divmod(s, 2)
                P = PA if s % 2 == 0 else PB
                dfix = (u // 8 == th)
                em = None
                for j in range(2):
                    em = top(lambda E, u=u, th=th, P=P, dfix=dfix, j=j: E.matmul(
                        P.ap()[:, ts(j, 512)], K98[0:98, ts(u, 128)],
                        Q98[0:98, th * 1024 + j * 512: th * 1024 + (j + 1) * 512],
                        start=True, stop=not (dfix and (u * 128 - th * 1024) // 512 == j)),
                        waits=[expsB[0], expsB[1], ed3, eimv2,
                               ev_expB.get(s - 2)])
                if dfix:
                    off = u * 128 - th * 1024
                    edg = vop(lambda E, u=u: E.tensor_scalar(
                        DGR, ID128F, SM2[:, u:u + 1], None, OP.mult),
                        waits=[eimv2, ev_expB.get(s - 2)])
                    em = top(lambda E, off=off, P=P: E.matmul(
                        P.ap()[:, off:off + 128], DGR, EYE2,
                        start=False, stop=True), waits=[edg])
                if s >= 1:
                    emit_avB(s - 1)
                ev_expB[s] = sop(lambda E, P=P, s=s: E.activation(
                    ET[:, (s % 2) * 1024: (s % 2 + 1) * 1024],
                    P.ap()[:, :], AF.Exp, bias=SHIFT2, scale=0.125),
                    waits=[em, ev_avB.get(s - 2)])
            emit_avB(2 * NT - 1)
            e = None
            for qh in range(2):
                e = vop(lambda E, qh=qh, c=c, b0=b0, e=e: E.tensor_copy(
                    CXT[b0:b0 + 64, c * S + qh * 1024: c * S + (qh + 1) * 1024],
                    PC.ap()[0:64, ts(qh, 1024)]),
                    waits=[ev_avB[2 * NT - 2], ev_avB[2 * NT - 1], e])
            st["evac"][h] = e
            st["explast"][h] = ev_expB[2 * NT - 1]
            return e

        def emit_attention(kind, sq, ev_x):
            masks = (kind == "attn_b")
            wq, wk, wv, wo = (wdr["wq_" + kind[-1]], wdr["wk_" + kind[-1]],
                              wdr["wv_" + kind[-1]], wdr["wo_" + kind[-1]])
            st["vdone"] = {}
            st["evac"] = {}
            st["explast"] = {}
            st["cnt"] = {}
            ev_xnt = []
            ln_batch([ev_x])
            for t in range(NT):
                e = ln_norm_tile(t)
                ev_xnt.append(transpose_to(XNT, XN, t, e, t % 2,
                                           extra_waits=[ev_xnt[t - 2] if t >= 2 else None]))
            wv_evs = load_weight_full(wv, [ev_x])
            ev_vt = []
            for t in range(NT):
                P = PA if t % 2 == 0 else PB
                last = None
                for dc in range(NC):
                    for j in range(2):
                        last = top(lambda E, t=t, dc=dc, P=P, j=j: E.matmul(
                            P.ap()[:, ts(j, 512)],
                            XNT[:, dc * S + t * 128: dc * S + (t + 1) * 128],
                            WBF[:, dc * 1024 + j * 512: dc * 1024 + (j + 1) * 512],
                            start=(dc == 0), stop=(dc == NC - 1)),
                            waits=[ev_xnt[-1], wv_evs[dc],
                                   ev_vt[t - 2] if t >= 2 else None])
                ev_vt.append(vop(lambda E, t=t, P=P: E.tensor_copy(
                    VT.ap().rearrange("p (u q) -> p u q", q=65)
                    [:, t * 16:(t + 1) * 16, 0:64],
                    P.ap().rearrange("p (u q) -> p u q", q=64)), waits=[last]))
            ev_v_all = ev_vt[-1]

            ev_selq = None
            ev_ok98 = None
            if masks:
                ev_selq = vop(lambda E: E.tensor_copy(Q98[64:96, :], SELK),
                              waits=[ev_consts])
                ev_ok98 = vop(lambda E: E.memset(K98[96:98, :], 1.0))
            ev_wprev = wv_evs[-1]
            last_head_ev = None
            for c in range(NC):
                eq = load_weight_colpair(wq, c, WBF2, [ev_wprev])
                ek = load_weight_colpair(wk, c, WBF3, [eq])
                ev_wprev = ek
                ev_qp = ev_kp = None
                prev_pair_ev = [st["evac"].get(2 * c - 1), st["explast"].get(2 * c - 1)]
                for which, wbf, dst, evw in (("q", WBF2, QP, eq), ("k", WBF3, KP, ek)):
                    e = None
                    for th in range(2):
                        P = PA if th == 0 else PB
                        last = None
                        for dc in range(NC):
                            for j in range(2):
                                last = top(lambda E, wbf=wbf, dc=dc, th=th, P=P, j=j:
                                           E.matmul(
                                    P.ap()[:, ts(j, 512)], wbf[:, ts(dc, 128)],
                                    XNT[:, dc * S + th * 1024 + j * 512:
                                        dc * S + th * 1024 + (j + 1) * 512],
                                    start=(dc == 0), stop=(dc == NC - 1)),
                                    waits=[evw, ev_vt[-1]] + prev_pair_ev)
                        e = vop(lambda E, dst=dst, th=th, P=P: E.tensor_copy(
                            dst[:, ts(th, 1024)], P.ap()[:, :]), waits=[last])
                    if which == "q":
                        ev_qp = e
                    else:
                        ev_kp = e
                for h2 in range(2):
                    h = 2 * c + h2
                    if masks:
                        last_head_ev = attn_b_head(c, h2, h, ev_qp, ev_kp,
                                                   ev_selq, ev_ok98, ev_v_all)
                    else:
                        last_head_ev = attn_a_head(c, h2, h, ev_qp, ev_kp, ev_v_all)
                    if DBG_HEADS is not None and h + 1 >= DBG_HEADS:
                        break
                if DBG_HEADS is not None and 2 * c + 2 >= DBG_HEADS:
                    break
            if DBG_STOP is not None:
                return last_head_ev
            wo_evs = load_weight_full(wo, [last_head_ev])
            ev_res = None
            prev_add = {}
            for t in range(NT):
                P = PA if t % 2 == 0 else PB
                last = None
                for dc in range(NC):
                    for j in range(2):
                        last = top(lambda E, t=t, dc=dc, P=P, j=j: E.matmul(
                            P.ap()[:, ts(j, 512)],
                            CXT[:, dc * S + t * 128: dc * S + (t + 1) * 128],
                            WBF[:, dc * 1024 + j * 512: dc * 1024 + (j + 1) * 512],
                            start=(dc == 0), stop=(dc == NC - 1)),
                            waits=[last_head_ev, wo_evs[dc],
                                   st["explast"].get(NH - 1),
                                   prev_add.get(t - 2)])
                ev_res = vop(lambda E, t=t, P=P: E.tensor_tensor(
                    X[:, ts(t, 1024)], X[:, ts(t, 1024)], P.ap()[:, :], OP.add),
                    waits=[last])
                prev_add[t] = ev_res
            return ev_res

        def emit_ffn(sq, ev_x, final):
            ev_fw = build_ffn_weights(ev_x)
            ev_q = []
            ln_batch([ev_x])
            for t in range(NT):
                ln_norm_tile(t)
                vop(lambda E: E.tensor_reduce(SMALL[:, 208:209], XN[:, :], AX.X, OP.max,
                                              apply_absolute_value=True))
                vpad()
                vop(lambda E: E.tensor_scalar(SMALL[:, 209:210], SMALL[:, 208:209],
                                              1e-12, 1.0 / 127.0, OP.max, OP.mult))
                vpad()
                vop(lambda E: E.reciprocal(SMALL[:, 210:211], SMALL[:, 209:210]))
                vop(lambda E, t=t: E.tensor_copy(LSA[:, t:t + 1], SMALL[:, 209:210]))
                vop(lambda E: E.tensor_scalar(XSC[:, :], XN[:, :],
                                              SMALL[:, 210:211], None, OP.mult))
                vop(lambda E: E.tensor_scalar(XNB[:, 0:1024], XSC[:, :], 0.0, 128.0,
                                              OP.max, OP.add))
                vop(lambda E: E.tensor_scalar(MSCR[:, :], XSC[:, :], 0.0, -128.0,
                                              OP.min, OP.add))
                vop(lambda E: E.tensor_tensor(XNB[:, 0:1024], XNB[:, 0:1024], MSCR[:, :], OP.add))
                vop(lambda E: E.tensor_scalar(XNB[:, 0:1024], XNB[:, 0:1024], -127.0, 127.0,
                                              OP.max, OP.min))
                vop(lambda E: E.tensor_copy(XI[:, :], XNB[:, 0:1024]))
                eqf = vop(lambda E: E.tensor_copy(XSC[:, :], XI[:, :]))
                ev_q.append(transpose_to(XNT, XSC, t, eqf, t % 2,
                                         extra_waits=[ev_q[t - 2] if t >= 2 else None]))
            ev_res = None
            prev_stt = {}
            prev_dma = None
            for t in range(NT):
                P = PA if t % 2 == 0 else PB
                last = None
                for dc in range(NC):
                    for j in range(2):
                        last = top(lambda E, t=t, dc=dc, P=P, j=j: E.matmul(
                            P.ap()[:, ts(j, 512)],
                            XNT[:, dc * S + t * 128: dc * S + (t + 1) * 128],
                            QRHS[:, dc * 1024 + j * 512: dc * 1024 + (j + 1) * 512],
                            start=(dc == 0), stop=(dc == NC - 1)),
                            waits=[ev_q[-1], ev_fw, prev_stt.get(t - 2)])
                e1 = vop(lambda E, t=t, P=P: E.scalar_tensor_tensor(
                    XSC[:, :], P.ap()[:, :], LSA[:, t:t + 1], RSB[:, :],
                    OP.mult, OP.mult), waits=[last])
                prev_stt[t] = e1
                if final:
                    ev_res = vop(lambda E, t=t: E.tensor_tensor(
                        XOUTS[:, :], X[:, ts(t, 1024)], XSC[:, :], OP.add),
                        waits=[prev_dma])
                    prev_dma = sch.dma(
                        G, xout[sq * S + t * 128: sq * S + (t + 1) * 128, :],
                        XOUTS[:, :], waits=[ev_res])
                else:
                    ev_res = vop(lambda E, t=t: E.tensor_tensor(
                        X[:, ts(t, 1024)], X[:, ts(t, 1024)], XSC[:, :], OP.add))
            return prev_dma if final else ev_res

        ev_x = None
        for sq in range(nseq):
            for t in range(NT):
                evd = sch.dma(SY, XSC[:, :],
                              xin[sq * S + t * 128: sq * S + (t + 1) * 128, :],
                              waits=[ev_x])
                ev_x = vop(lambda E, t=t: E.tensor_copy(X[:, ts(t, 1024)], XSC[:, :]),
                           waits=[evd])
            if stages[0].startswith("dbg"):
                mode = stages[0][4:]
                if mode == "x":
                    for t in range(NT):
                        e1 = vop(lambda E, t=t: E.tensor_copy(XOUTS[:, :],
                                                              X[:, ts(t, 1024)]),
                                 waits=[ev_x])
                        sch.dma(G, xout[sq * S + t * 128: sq * S + (t + 1) * 128, :],
                                XOUTS[:, :], waits=[e1])
                        ev_x = vop(lambda E: E.tensor_copy(SMALL[0:1, 150:151],
                                                           SMALL[0:1, 150:151]),
                                   waits=[("dma_sync", sch.ctr["dma_sync"])])
                elif mode == "ffnw":
                    efw = build_ffn_weights(ev_x)
                    prevd = None
                    for dc in range(NC):
                        e1 = vop(lambda E, dc=dc: E.tensor_copy(
                            XOUTS[:, :], QRHS[:, ts(dc, 1024)]), waits=[efw, prevd])
                        prevd = sch.dma(G, xout[dc * 128:(dc + 1) * 128, :],
                                        XOUTS[:, :], waits=[e1])
                    e1 = vop(lambda E: E.tensor_copy(XOUTS[:, :], RSB[:, :]),
                             waits=[efw, prevd])
                    prevd = sch.dma(G, xout[8 * 128:9 * 128, :], XOUTS[:, :], waits=[e1])
                    ev_x = prevd
                elif mode == "ln":
                    ln_batch([ev_x])
                    for t in range(NT):
                        e = ln_norm_tile(t, extra=[ev_x])
                        sch.dma(G, xout[sq * S + t * 128: sq * S + (t + 1) * 128, :],
                                XN[:, :], waits=[e])
                        ev_x = vop(lambda E: E.tensor_copy(DPAD[:, 0:1], DPAD[:, 2:3]),
                                   waits=[("dma_sync", sch.ctr["dma_sync"])])
                elif mode in ("vproj", "qp", "kp", "et", "ctx"):
                    ev2 = emit_attention("attn_a", sq, ev_x)
                    src_map = {"vproj": VT, "ctx": None}
                    if mode == "vproj":
                        for t in range(NT):
                            e1 = vop(lambda E, t=t: E.tensor_copy(
                                XOUTS[:, :], VT[:, ts(t, 1024)]), waits=[ev2])
                            sch.dma(G, xout[sq * S + t * 128: sq * S + (t + 1) * 128, :],
                                    XOUTS[:, :], waits=[e1])
                            ev2 = vop(lambda E: E.tensor_copy(SMALL[0:1, 150:151],
                                                              SMALL[0:1, 150:151]),
                                      waits=[("dma_sync", sch.ctr["dma_sync"])])
                    elif mode in ("qp", "kp"):
                        srcb = QP if mode == "qp" else KP
                        for j in range(2):
                            e1 = vop(lambda E, j=j, srcb=srcb: E.tensor_copy(
                                XOUTS[:, :], srcb[:, ts(j, 1024)]), waits=[ev2])
                            sch.dma(G, xout[j * 128:(j + 1) * 128, :], XOUTS[:, :],
                                    waits=[e1])
                            ev2 = vop(lambda E: E.tensor_copy(SMALL[0:1, 150:151],
                                                              SMALL[0:1, 150:151]),
                                      waits=[("dma_sync", sch.ctr["dma_sync"])])
                    elif mode == "et":
                        for j in range(2):
                            e1 = vop(lambda E, j=j: E.tensor_copy(
                                XOUTS[:, :], ET[:, ts(j, 1024)]), waits=[ev2])
                            sch.dma(G, xout[j * 128:(j + 1) * 128, :], XOUTS[:, :],
                                    waits=[e1])
                            ev2 = vop(lambda E: E.tensor_copy(SMALL[0:1, 150:151],
                                                              SMALL[0:1, 150:151]),
                                      waits=[("dma_sync", sch.ctr["dma_sync"])])
                    elif mode == "ctx":
                        for c in range(NC):
                            for j in range(2):
                                e1 = vop(lambda E, c=c, j=j: E.tensor_copy(
                                    XOUTS[:, :], CXT[:, c * S + j * 1024:
                                                     c * S + (j + 1) * 1024]),
                                    waits=[ev2])
                                sch.dma(G, xout[(c * 2 + j) * 128:
                                                (c * 2 + j + 1) * 128, :],
                                        XOUTS[:, :], waits=[e1])
                                ev2 = vop(lambda E: E.tensor_copy(
                                    SMALL[0:1, 150:151], SMALL[0:1, 150:151]),
                                    waits=[("dma_sync", sch.ctr["dma_sync"])])
                    ev_x = ev2
                continue
            for st_i, stage in enumerate(stages):
                final = (st_i == len(stages) - 1)
                if stage.startswith("attn"):
                    ev_x = emit_attention(stage, sq, ev_x)
                    if final:
                        prev_dma = None
                        for t in range(NT):
                            e1 = vop(lambda E, t=t: E.tensor_copy(
                                XOUTS[:, :], X[:, ts(t, 1024)]),
                                waits=[ev_x, prev_dma])
                            prev_dma = sch.dma(
                                G, xout[sq * S + t * 128: sq * S + (t + 1) * 128, :],
                                XOUTS[:, :], waits=[e1])
                        ev_x = prev_dma
                else:
                    ev_x = emit_ffn(sq, ev_x, final)

        sch.op(G, lambda E: E.memset(SMB[0:1, 400:404], 0.0),
               waits=[("dma_sync", sch.ctr["dma_sync"]), ("v", sch.ctr["v"]),
                      ("t", sch.ctr["t"]), ("s", sch.ctr["s"])])
        sch.align_dma_waits()
        sch.check()
        sch.emit(blk)
    return nc


_NC_CACHE = {}


def _get_program(stages, nseq):
    key = (tuple(stages), nseq)
    if key not in _NC_CACHE:
        _NC_CACHE[key] = build_program(stages, nseq)
    return _NC_CACHE[key]


def kernel(L0_attn_a_hidden_states, L0_attn_a_Wq, L0_attn_a_Wk, L0_attn_a_Wv,
           L0_attn_a_Wo, L0_attn_b_wq, L0_attn_b_wk, L0_attn_b_wv, L0_attn_b_wo,
           L0_ffn_rhs, _stages=("attn_a", "ffn", "attn_b", "ffn")):
    x = np.ascontiguousarray(np.asarray(L0_attn_a_hidden_states, dtype=np.float32))
    B = x.shape[0]
    xf = x.reshape(B * 2048, D)
    cf, cb = host_consts()
    wT = {
        "wq_a": np.ascontiguousarray(np.asarray(L0_attn_a_Wq, np.float32).T),
        "wk_a": np.ascontiguousarray(np.asarray(L0_attn_a_Wk, np.float32).T),
        "wv_a": np.ascontiguousarray(np.asarray(L0_attn_a_Wv, np.float32).T),
        "wo_a": np.ascontiguousarray(np.asarray(L0_attn_a_Wo, np.float32).T),
        "wq_b": np.ascontiguousarray(np.asarray(L0_attn_b_wq, np.float32).T),
        "wk_b": np.ascontiguousarray(np.asarray(L0_attn_b_wk, np.float32).T),
        "wv_b": np.ascontiguousarray(np.asarray(L0_attn_b_wv, np.float32).T),
        "wo_b": np.ascontiguousarray(np.asarray(L0_attn_b_wo, np.float32).T),
    }
    rhsT = np.ascontiguousarray(np.asarray(L0_ffn_rhs, np.float32).T)
    nseq = B // 8
    ncore = _get_program(_stages, nseq)
    in_maps = []
    for core in range(8):
        m = {"xin": np.ascontiguousarray(xf[core * nseq * S:(core + 1) * nseq * S, :]),
             "frhsT": rhsT, "cf": cf, "cb": cb}
        m.update(wT)
        in_maps.append(m)
    res = run_bass_kernel_spmd(ncore, in_maps, core_ids=list(range(8)))
    out = np.concatenate([np.asarray(r["xout"]) for r in res.results], axis=0)
    return out.reshape(B, 2048, D)
